# version 05
# baseline (speedup 1.0000x reference)
import dataclasses
import numpy as np
import ml_dtypes

import concourse.bass as bass
import concourse.mybir as mybir
from concourse.bass_utils import run_bass_kernel_spmd

F32 = mybir.dt.float32
BF16 = mybir.dt.bfloat16
AF = mybir.ActivationFunctionType
ALU = mybir.AluOpType
AX = mybir.AxisListType

NCORES = 8
L, D = 8192, 1024
NHP = 8
NCH = 64
NT = 8
NQT = 4
NEAR = (0, 1, 2, 3, 63)

DEBUG = False
PHASES = 4
SUBPH = 99
PH2N = 8
PH2M = 9
PH2X = 7


def _bucket(d):
    n = np.asarray(d, dtype=np.int64)
    is_small = n < 16
    nf = np.maximum(n, 1).astype(np.float64)
    val = 16 + np.trunc(np.log(nf / 16.0) / np.log(8.0) * 16.0).astype(np.int64)
    val = np.minimum(val, 31)
    ret = np.where(is_small, n, val)
    return np.maximum(ret, 0)


def _m_fwd():
    i = np.arange(16384)
    d = i - 8192
    bk = np.where(d <= 0, 0, _bucket(d))
    M = np.zeros((32, 16384), np.float32)
    M[bk, i] = 8.0
    M[0, :] -= 8.0
    return M


def _m_rev(t0):
    y = np.arange(8192)
    dy = (t0 - 1) - ((y + t0) % 8192)
    bk = np.where(dy <= 0, 0, _bucket(dy))
    M = np.zeros((32, 8192), np.float32)
    M[bk, y] = 8.0
    M[0, :] -= 8.0
    if t0 == 0:
        M[:] = 0.0
    return M


def _clsmask(j):
    m = np.zeros((4, 64), np.float32)
    for qt in range(4):
        s = 16 * j + 4 * qt
        for c2 in range(64):
            if c2 not in NEAR and c2 + s >= 64:
                m[qt, c2] = 1.0
    return np.ascontiguousarray(m.reshape(1, 256))


def build_program(debug=False, phases=3):
    nc = bass.Bass("TRN2", target_bir_lowering=False, debug=False)

    xb = nc.dram_tensor("xb", [L, D], F32, kind="ExternalInput").ap()
    wqT = nc.dram_tensor("wqT", [D, D], F32, kind="ExternalInput").ap()
    wkT = nc.dram_tensor("wkT", [D, D], F32, kind="ExternalInput").ap()
    wvT = nc.dram_tensor("wvT", [D, D], F32, kind="ExternalInput").ap()
    woT = nc.dram_tensor("woT", [D, D], F32, kind="ExternalInput").ap()
    wrT = nc.dram_tensor("wrT", [D, D], F32, kind="ExternalInput").ap()
    wkfT = nc.dram_tensor("wkfT", [D, D], F32, kind="ExternalInput").ap()
    wvfT = nc.dram_tensor("wvfT", [D, D], F32, kind="ExternalInput").ap()
    vecs_in = nc.dram_tensor("vecs", [128, 56], F32, kind="ExternalInput").ap()
    bvrow_in = nc.dram_tensor("bvrow", [1, D], F32, kind="ExternalInput").ap()
    relb_in = nc.dram_tensor("relb", [32, 16], F32, kind="ExternalInput").ap()
    mf_in = nc.dram_tensor("mf", [32, 16384], F32, kind="ExternalInput").ap()
    mr_in = nc.dram_tensor("mr", [32, 8192], F32, kind="ExternalInput").ap()
    clsm_in = nc.dram_tensor("clsm", [1, 256], F32, kind="ExternalInput").ap()
    ident_in = nc.dram_tensor("identb", [128, 128], BF16, kind="ExternalInput").ap()
    y_out = nc.dram_tensor("y", [2048, D], F32, kind="ExternalOutput").ap()

    ktd = nc.dram_tensor("ktd", [NHP, 128, 2 * L], BF16, kind="Internal").ap()
    vsw = nc.dram_tensor("vsw", [2 * NCH, 128, 1040], BF16, kind="Internal").ap()
    xlntd = nc.dram_tensor("xlntd", [8, 128, L], BF16, kind="Internal").ap()
    p8f = nc.dram_tensor("p8f", [16, 16384], F32, kind="Internal").ap()
    p8r = nc.dram_tensor("p8r", [16, 8192], F32, kind="Internal").ap()
    x2d = nc.dram_tensor("x2d", [2052, D], F32, kind="Internal").ap()
    xsd = nc.dram_tensor("xsd", [2052, D], F32, kind="Internal").ap()
    w3d = nc.dram_tensor("w3d", [4, 128, 8, D], BF16, kind="Internal").ap()

    dbg = {}
    if debug:
        def dout(name, shape, dt):
            dbg[name] = nc.dram_tensor("dbg_" + name, shape, dt,
                                       kind="ExternalOutput").ap()
        dout("xlnt", [8, 128, L], BF16)
        dout("qt", [128, 8, 2052], BF16)
        dout("kt", [NHP, 128, 2 * L], BF16)
        dout("vsw", [2 * NCH, 128, 1040], BF16)
        dout("p8f", [16, 16384], F32)
        dout("p8r", [16, 8192], F32)
        dout("sc", [1, 256], F32)
        dout("vtab", [128, 2048], F32)
        dout("vtab1", [128, 2048], F32)
        dout("atn", [128, 8, 2052], BF16)
        dout("x2", [2049, D], F32)
        dout("c8dbg", [128, 128], F32)

    from contextlib import ExitStack
    ctx = ExitStack()
    sbT = lambda *a: ctx.enter_context(nc.sbuf_tensor(*a))
    psT = lambda *a: ctx.enter_context(nc.psum_tensor(*a))

    qts = sbT("qts", [128, 8, 2052], BF16)
    atn = sbT("atn", [128, 8, 2052], BF16)
    pool = sbT("pool", [128, 57344], BF16)
    ptb = sbT("ptb", [128, 2, 2, 512], BF16)
    vecs = sbT("vecs_s", [128, 56], F32)
    identb = sbT("ident_s", [128, 128], BF16)
    relb = sbT("relb_s", [32, 16], F32)
    relbb = sbT("relbb_s", [32, 16], BF16)
    clsm = sbT("clsm_s", [1, 256], F32)
    bvrb = sbT("bvrb", [1, D], BF16)
    sums = sbT("sums", [128, 8, 2], F32)
    stats = sbT("stats", [128, 48], F32)
    vtab = sbT("vtab", [128, 2048], F32)
    vtab1 = sbT("vtab1", [128, 2048], F32)
    c8dbg = sbT("c8dbg", [128, 128], F32)
    srow = sbT("srow", [1, 256], F32)
    srow2 = sbT("srow2", [1, 256], F32)
    onesf = sbT("onesf", [1, 128], F32)
    onesb = sbT("onesb", [1, 128], BF16)
    onescol = sbT("onescol", [128, 1], BF16)
    onesw = sbT("onesw", [128, 64], BF16)
    mkac = sbT("mkac", [1, 80], F32)
    colv = sbT("colv", [128, 4], F32)
    colvb = sbT("colvb", [128, 4], BF16)
    bb_st = sbT("bb_st", [128, 128], F32)
    ptbd = sbT("ptbd", [128, 128], BF16)

    pf32 = pool[:].bitcast(F32)
    pool_flat = pool[:]
    mrow = pf32[0:32, 3072:3584]
    mrowb = pool_flat[0:32, 8192:8704]
    prow = pf32[0:16, 3584:4096]
    rbB = pf32[:, 22656:23168]
    bbld = pf32[0:2, 23168:24192]
    bbldb = pool_flat[0:2, 48384:49408]

    P1_XF32 = 0
    P1_XLN = 16384
    P1_XLNT = 24576
    P1_WVT = 32768
    P1_VST = 40960
    P1_KST = 42048
    P1_QSL = 0

    P2_KT = 0
    P2_V = 16384
    P2_B8F = 16512

    P3_W = 0
    P3_RF = 16384
    P3_XR = 40960
    P3_XK = 45056
    P3_KA = 45056
    P3_TMPF = 24576

    s_a = psT("s_a", [128, 2, 512], F32)
    s_b = psT("s_b", [128, 2, 512], F32)
    pv0 = psT("pv0", [65, 512], F32)
    pv1 = psT("pv1", [65, 512], F32)
    bndp = psT("bndp", [128, 512], F32)
    spare = psT("spare", [128, 512], F32)

    NSLOT = 6
    sem_names = (["mm", "ac", "dv", "st", "ini", "sx", "pl"]
                 + [f"sy{i}" for i in range(NSLOT)]
                 + [f"gp{i}" for i in range(NSLOT)])
    sems = {k: ctx.enter_context(nc.semaphore(k)) for k in sem_names}
    block = ctx.enter_context(nc.Block())

    atn_b = atn[:].rearrange("p a b -> p (a b)")

    def walk(me, eng):
        S = sems
        cnt = {k: 0 for k in sem_names}

        def inc(key, amt=1):
            cnt[key] += amt
            return cnt[key]

        def on(name):
            return me == name

        def W(key, val=None):
            if val is None:
                key, val = key
            if val > 0:
                eng.wait_ge(S[key], val)

        dma_rr = {"sy": 0, "gp": 0}
        slot_uses = {k: 0 for k in sem_names}

        def dma(queue, out, in_):
            name = f"{queue}{dma_rr[queue] % NSLOT}"
            dma_rr[queue] += 1
            if on(queue):
                if slot_uses[name] > 0:
                    W(name, 16 * slot_uses[name])
                q = eng
                q.dma_start(out=out, in_=in_).then_inc(S[name], 16)
            slot_uses[name] += 1
            cnt[name] = 16 * slot_uses[name]
            return (name, 16 * slot_uses[name])

        rt = {}

        def get_rt():
            if "t0" not in rt:
                pid = eng.partition_id()
                j = pid % 4
                rt["j"] = j
                rt["t0"] = j * 2048
                rt["qoff"] = j * 2048 - (j >= 1)
            return rt

        def guard(key):
            pass

        def act_copy(dst, src, bias=None, wait=()):
            if on("ac"):
                for k, v in wait:
                    W(k, v)
                if bias is None:
                    eng.activation(dst, src, AF.Identity).then_inc(S["ac"], 1)
                else:
                    eng.activation(dst, src, AF.Identity,
                                   bias=bias).then_inc(S["ac"], 1)
            return inc("ac")

        def part_row(src_col_f32, spare_off, wait_dv):
            ci = spare_off % 4
            if on("dv"):
                W("dv", wait_dv)
                eng.tensor_copy(colvb[:, ci:ci + 1], src_col_f32).then_inc(S["dv"], 1)
            dvt = inc("dv")
            if on("mm"):
                W("dv", dvt)
                eng.matmul(spare[0:1, 128 * ci:128 * ci + 128], colvb[:, ci:ci + 1],
                           identb[:], start=True, stop=True).then_inc(S["mm"], 1)
            mmt = inc("mm")
            return mmt, spare[0:1, 128 * ci:128 * ci + 128]

        def dma_ini(out, in_):
            if on("sy"):
                eng.dma_start(out=out, in_=in_).then_inc(S["ini"], 16)
            return inc("ini", 16)

        dma_ini(vecs[:], vecs_in)
        dma_ini(identb[:], ident_in)
        dma_ini(relb[:], relb_in)
        dma_ini(clsm[:], clsm_in)
        ld1 = dma_ini(pf32[0:1, 2048:3072], bvrow_in)
        if on("sy"):
            r = get_rt()
            s1 = dataclasses.replace(xb[0:2048, :], offset=xb.offset + r["t0"] * 1024)
            eng.dma_start(out=xsd[1:2049, :], in_=s1).then_inc(S["sx"], 16)
            s0 = dataclasses.replace(xb[0:1, :], offset=xb.offset + r["qoff"] * 1024)
            eng.dma_start(out=xsd[0:1, :], in_=s0).then_inc(S["sx"], 16)
        xsd_w = inc("sx", 32)

        if on("dv"):
            W("ini", ld1)
            eng.tensor_copy(relbb[:], relb[:])
            eng.tensor_copy(bvrb[:], pf32[0:1, 2048:3072])
            eng.memset(onesf[:], 1.0)
            eng.memset(onesb[:], 1.0)
            eng.memset(onescol[:], 1.0)
            eng.memset(onesw[:], 1.0)
            eng.memset(srow[:], 0.0)
            eng.memset(srow2[:], 0.0)
            eng.memset(c8dbg[:], 0.0)
            eng.memset(colv[:], 0.0)
            eng.memset(qts[:], 0.0)
            eng.memset(atn[:], 0.0)
            eng.memset(mkac[:], 0.0).then_inc(S["dv"], 1)
        dv_init = inc("dv")

        ppst = {"cst": dv_init, "ev": 0, "pdma": ("ini", 0), "mmw": 0}

        def profile_pass(m_src, n_chunks, dst):
            for ch in range(n_chunks):
                if on("sy"):
                    W("dv", ppst["cst"])
                w = dma("sy", mrow, m_src[:, 512 * ch:512 * ch + 512])
                if on("dv"):
                    W(w)
                    W("mm", ppst["mmw"])
                    eng.tensor_copy(mrowb, mrow).then_inc(S["dv"], 1)
                ppst["cst"] = inc("dv")
                if on("mm"):
                    W("dv", ppst["cst"])
                    W("ac", ppst["ev"])
                    eng.matmul(spare[0:16, :], relbb[:], mrowb,
                               start=True, stop=True).then_inc(S["mm"], 1)
                mmc = inc("mm")
                ppst["mmw"] = mmc
                ppst["ev"] = act_copy(prow, spare[0:16, :],
                                      wait=[("mm", mmc), ppst["pdma"]])
                if on("gp"):
                    W("ac", ppst["ev"])
                ppst["pdma"] = dma("gp", dst[:, 512 * ch:512 * ch + 512], prow)
            return cnt["ac"]

        if SUBPH < 2:
            return finish(me, eng, cnt, S, [])
        if SUBPH == 33:
            profile_pass(mf_in, 16, p8f)
            return finish(me, eng, cnt, S, [])
        if SUBPH == 34:
            profile_pass(mf_in, 8, p8f)
            profile_pass(mf_in, 8, p8f)
            return finish(me, eng, cnt, S, [])
        if SUBPH == 35:
            profile_pass(mf_in, 24, p8f)
            return finish(me, eng, cnt, S, [])
        if SUBPH == 36:
            profile_pass(mf_in, 32, p8f)
            profile_pass(mf_in, 2, p8f)
            return finish(me, eng, cnt, S, [])
        pp1 = profile_pass(mf_in, 32, p8f)
        if SUBPH < 3:
            return finish(me, eng, cnt, S, [])
        if SUBPH == 31:
            pp2 = profile_pass(mf_in, 16, p8f)
            return finish(me, eng, cnt, S, [])
        if SUBPH == 32:
            pp2 = profile_pass(mr_in, 16, p8f)
            return finish(me, eng, cnt, S, [])
        pp2 = profile_pass(mr_in, 16, p8r)
        if SUBPH < 4:
            return finish(me, eng, cnt, S, [])

        cast_w_dv = ppst["cst"]
        for wi, wsrc in enumerate([wqT, wkT, wvT]):
            for cc in range(8):
                if on("sy"):
                    W("dv", cast_w_dv)
                w = dma("sy", pf32[:, 0:1024],
                        wsrc[128 * cc:128 * cc + 128, :])
                if on("dv"):
                    W(w)
                    if wi < 2:
                        dst = atn_b[:, 8192 * wi + 1024 * cc:8192 * wi + 1024 * cc + 1024]
                    else:
                        dst = pool_flat[:, P1_WVT + 1024 * cc:P1_WVT + 1024 * cc + 1024]
                    eng.tensor_copy(dst, pf32[:, 0:1024]).then_inc(S["dv"], 1)
                cast_w_dv = inc("dv")

        if SUBPH < 5:
            return finish(me, eng, cnt, S, [])
        w3prev = {"cst": cast_w_dv, "dma": ("ini", 0)}
        for wi, wsrc in enumerate([woT, wrT, wkfT, wvfT]):
            for cc in range(8):
                if on("sy"):
                    W("dv", w3prev["cst"])
                w = dma("sy", pf32[:, 1024:2048], wsrc[128 * cc:128 * cc + 128, :])
                if on("dv"):
                    W(w)
                    W(w3prev["dma"])
                    eng.tensor_copy(pool_flat[:, P1_KST:P1_KST + 1024],
                                    pf32[:, 1024:2048]).then_inc(S["dv"], 1)
                cst = inc("dv")
                w3prev["cst"] = cst
                if on("gp"):
                    if wi == 0 and cc == 0:
                        W("ac", pp2)
                    W("dv", cst)
                w3prev["dma"] = dma("gp", w3d[wi, :, cc, :],
                                    pool_flat[:, P1_KST:P1_KST + 1024])

        wcast_dv = cnt["dv"]
        w3_dma_last = w3prev["dma"]

        if phases < 1:
            return finish(me, eng, cnt, S, [])

        xf32v = pf32[:, 0:8192].rearrange("p (a c) -> p a c", c=1024)
        xlnv = pool_flat[:, P1_XLN:P1_XLN + 8192].rearrange("p (a c) -> p a c", c=1024)
        xlntv = pool_flat[:, P1_XLNT:P1_XLNT + 8192].rearrange("p (a c) -> p a c", c=1024)
        wvtv = pool_flat[:, P1_WVT:P1_WVT + 8192].rearrange("p (a c) -> p a c", c=1024)
        vstv = pool_flat[:, P1_VST:P1_VST + 1040].rearrange("p (h e) -> p h e", e=65)
        vst_flat = pool_flat[:, P1_VST:P1_VST + 1040]
        kstv = pool_flat[:, P1_KST:P1_KST + 1024]
        wqTb = atn_b[:, 0:8192].rearrange("p (cc d) -> p cc d", d=1024)
        wkTb = atn_b[:, 8192:16384].rearrange("p (cc d) -> p cc d", d=1024)

        def ln_tokenmajor(src_fn, n_sub, dst_fn, first_wait,
                          dummy=None):
            for a in range(n_sub):
                if on("ac"):
                    if a == 0:
                        for k, v in first_wait:
                            W(k, v)
                    eng.activation(dummy(a, 0), src_fn(a), AF.Identity,
                                   accum_out=sums[:, a, 0:1])
                    ins = eng.activation(dummy(a, 1), src_fn(a), AF.Square,
                                         accum_out=sums[:, a, 1:2])
                    if a == n_sub - 1:
                        ins.then_inc(S["ac"], 1)
            acc = inc("ac")
            if on("dv"):
                W("ac", acc)
                eng.tensor_scalar(stats[:, 0:n_sub], sums[:, 0:n_sub, 0],
                                  -1.0 / 1024, None, op0=ALU.mult)
                eng.tensor_scalar(stats[:, 8:8 + n_sub], sums[:, 0:n_sub, 1],
                                  1.0 / 1024, None, op0=ALU.mult).then_inc(S["dv"], 1)
            d1 = inc("dv")
            if on("dv"):
                W("dv", d1)
                eng.tensor_tensor(stats[:, 16:16 + n_sub], stats[:, 0:n_sub],
                                  stats[:, 0:n_sub], op=ALU.mult).then_inc(S["dv"], 1)
            d2 = inc("dv")
            if on("dv"):
                W("dv", d2)
                eng.scalar_tensor_tensor(stats[:, 8:8 + n_sub], stats[:, 8:8 + n_sub],
                                         1e-5, stats[:, 16:16 + n_sub],
                                         op0=ALU.add, op1=ALU.subtract).then_inc(S["dv"], 1)
            d3 = inc("dv")
            if on("ac"):
                W("dv", d3)
                eng.activation(stats[:, 24:24 + n_sub], stats[:, 8:8 + n_sub],
                               AF.Ln).then_inc(S["ac"], 1)
            a1 = inc("ac")
            if on("ac"):
                W("ac", a1)
                eng.activation(stats[:, 16:16 + n_sub], stats[:, 24:24 + n_sub],
                               AF.Exp, scale=-0.5).then_inc(S["ac"], 1)
            a2 = inc("ac")
            for a in range(n_sub):
                if on("dv"):
                    if a == 0:
                        W("ac", a2)
                    ins = eng.tensor_scalar(dst_fn(a), src_fn(a),
                                            stats[:, a:a + 1],
                                            stats[:, 16 + a:17 + a],
                                            op0=ALU.add, op1=ALU.mult)
                    if a == n_sub - 1:
                        ins.then_inc(S["dv"], 1)
            return inc("dv")

        def transpose8(src_fn, dst_fn, n_blocks, entry_ac):
            evs = {}
            ng = (n_blocks + 3) // 4
            for g in range(ng):
                sz = min(4, n_blocks - 4 * g)
                bank = bndp if g % 2 == 0 else spare
                if on("mm"):
                    if g < 2:
                        W("ac", entry_ac)
                    else:
                        W("ac", evs[g - 2])
                for i in range(sz):
                    if on("mm"):
                        ins = eng.matmul(bank[:, 128 * i:128 * i + 128],
                                         src_fn(4 * g + i), identb[:],
                                         start=True, stop=True)
                        if i == sz - 1:
                            ins.then_inc(S["mm"], 1)
                mmc = inc("mm")
                evs[g] = act_copy(
                    dst_fn(g, sz),
                    bank[:, 0:128 * sz].rearrange("p (cc t) -> p cc t", t=128),
                    wait=[("mm", mmc)])
            return evs[ng - 1]

        prev = {"kev": 0, "kdma": (), "kgp": 0, "vev": 0, "vdma": (),
                "xlnt_dma": ("ini", 0), "ln_done": 0, "mm_tile": 0}

        for T in range(NT):
            if on("sy"):
                if T > 0:
                    W("mm", prev["mm_tile"])
                    W("dv", prev["ln_done"])
                else:
                    W("dv", wcast_dv)
                eng.dma_start(out=xf32v[:, :, :],
                              in_=xb[1024 * T:1024 * T + 1024, :].rearrange(
                                  "(a p) c -> p a c", p=128)).then_inc(S["sx"], 16)
            xw = inc("sx", 16)

            first_w = [("sx", xw)]
            if T == 0:
                first_w.append(("dv", wcast_dv))
            ln_done = ln_tokenmajor(lambda a: xf32v[:, a, :], 8,
                                    lambda a: xlnv[:, a, :], first_w,
                                    dummy=lambda a, k: qts[:, a, 1024 * k:1024 * k + 1024])
            prev["ln_done"] = ln_done

            def t_src(blk):
                a, ccb = divmod(blk, 8)
                return xlnv[:, a, 128 * ccb:128 * ccb + 128]

            def t_dst(g, sz):
                blk0 = 4 * g
                a, ccb0 = divmod(blk0, 8)
                return xlntv[:, ccb0:ccb0 + sz, 128 * a:128 * a + 128]

            if on("mm"):
                W("dv", ln_done)
            if on("ac") and T > 0:
                W(prev["xlnt_dma"])
            tr_done = transpose8(t_src, t_dst, 64, entry_ac=cnt["ac"])

            if on("gp"):
                W("ac", tr_done)
            prev["xlnt_dma"] = dma(
                "gp",
                xlntd[:, :, 1024 * T:1024 * T + 1024].rearrange("cc p c -> p cc c"),
                xlntv[:])

            for hp in range(NHP):
                if on("mm"):
                    if hp == 0:
                        W("ac", tr_done)
                    else:
                        W("ac", prev["kev"])
                for h2 in range(2):
                    for cc in range(8):
                        if on("mm"):
                            ins = eng.matmul(
                                s_a[:, h2, :],
                                wkTb[:, cc, 128 * hp:128 * hp + 128],
                                xlntv[:, cc, 512 * h2:512 * h2 + 512],
                                start=(cc == 0), stop=(cc == 7))
                            if h2 == 1 and cc == 7:
                                ins.then_inc(S["mm"], 1)
                mmk = inc("mm")
                if on("ac"):
                    W("mm", mmk)
                    if hp > 0 or T > 0:
                        for t_ in prev["kdma"]:
                            W(t_)
                        W("pl", prev["kgp"])
                    else:
                        W(w3_dma_last)
                    eng.activation(kstv[0:128, 0:512], s_a[:, 0, :], AF.Identity,
                                   bias=vecs[:, 8 + hp:9 + hp])
                    eng.activation(kstv[0:128, 512:1024], s_a[:, 1, :], AF.Identity,
                                   bias=vecs[:, 8 + hp:9 + hp]).then_inc(S["ac"], 1)
                prev["kev"] = inc("ac")
                if on("gp"):
                    W("ac", prev["kev"])
                    eng.tensor_reduce(mkac[0:1, 8 * T + hp:8 * T + hp + 1],
                                      kstv[0:128, :], AX.XYZWC, ALU.max,
                                      apply_absolute_value=True).then_inc(S["pl"], 1)
                prev["kgp"] = inc("pl", 1)
                if on("sy"):
                    W("ac", prev["kev"])
                kd1 = dma("sy", ktd[hp, :, 1024 * T:1024 * T + 1024], kstv[0:128, :])
                kd2 = dma("sy", ktd[hp, :, L + 1024 * T:L + 1024 * T + 1024],
                          kstv[0:128, :])
                prev["kdma"] = (kd1, kd2)

            for a in range(8):
                if on("mm") and (T > 0 or a > 0):
                    W("dv", prev["vev"])
                for dh in range(2):
                    for cc in range(8):
                        if on("mm"):
                            eng.matmul(
                                s_b[:, dh, :],
                                xlntv[:, cc, 128 * a:128 * a + 128],
                                wvtv[:, cc, 512 * dh:512 * dh + 512],
                                start=(cc == 0), stop=False)
                    if on("mm"):
                        ins = eng.matmul(s_b[:, dh, :], onesb[:],
                                         bvrb[0:1, 512 * dh:512 * dh + 512],
                                         start=False, stop=True)
                        if dh == 1:
                            ins.then_inc(S["mm"], 1)
                mmv = inc("mm")
                if on("dv"):
                    W("mm", mmv)
                    if T > 0 or a > 0:
                        for t_ in prev["vdma"]:
                            W(t_)
                    if T == 0 and a == 0:
                        eng.memset(vstv[:, :, 64:65], 1.0)
                    eng.tensor_copy(
                        vstv[:, :, 0:64],
                        s_b[:].rearrange("p dh c -> p (dh c)").rearrange(
                            "p (h e) -> p h e", e=64)).then_inc(S["dv"], 1)
                prev["vev"] = inc("dv")
                if on("gp"):
                    W("dv", prev["vev"])
                chunk = 8 * T + a
                vd1 = dma("gp", vsw[chunk, :, :], vst_flat)
                vd2 = dma("gp", vsw[chunk + 64, :, :], vst_flat)
                prev["vdma"] = (vd1, vd2)

            prev["mm_tile"] = cnt["mm"]


        ph1_done = dict(cnt)

        if phases < 2:
            keys = ["xlnt", "kt", "vsw", "p8f", "p8r"] if debug else []
            return finish(me, eng, cnt, S, keys)

        qslv = pool_flat[:, P1_QSL:P1_QSL + 16416].rearrange(
            "p (cc c) -> p cc c", c=2052)
        if on("sy"):
            r = get_rt()
            src = xlntd[:, :, 0:2048]
            src = dataclasses.replace(src, offset=src.offset + r["t0"])
            W(prev["xlnt_dma"])
            eng.dma_start(out=qslv[:, :, 1:2049],
                          in_=src.rearrange("cc p c -> p cc c")).then_inc(S["sx"], 16)
            src0 = xlntd[:, :, 0:1]
            src0 = dataclasses.replace(src0, offset=src0.offset + r["qoff"])
            eng.dma_start(out=qslv[:, :, 0:1],
                          in_=src0.rearrange("cc p c -> p cc c"),
                          allow_slow_non_contiguous=True).then_inc(S["sx"], 16)
        qsl_w = inc("sx", 32)

        qev = 0
        for hp in range(NHP):
            segs = [(0, 512), (512, 512), (1024, 512), (1536, 512), (2048, 1)]
            for s0_, sw in segs:
                if on("mm"):
                    if hp == 0 and s0_ == 0:
                        W("sx", qsl_w)
                        W("ac", ph1_done["ac"])
                    else:
                        W("ac", qev)
                    for cc in range(8):
                        ins = eng.matmul(s_a[:, 0, 0:sw],
                                         wqTb[:, cc, 128 * hp:128 * hp + 128],
                                         qslv[:, cc, s0_:s0_ + sw],
                                         start=(cc == 0), stop=(cc == 7))
                        if cc == 7:
                            ins.then_inc(S["mm"], 1)
                else:
                    pass
                mmq = inc("mm")
                qev = act_copy(qts[:, hp, s0_:s0_ + sw], s_a[:, 0, 0:sw],
                               bias=vecs[:, hp:hp + 1], wait=[("mm", mmq)])
        q_ac = qev
        if SUBPH == 6:
            return finish(me, eng, cnt, S, [])

        if on("ac"):
            W("ac", q_ac)
            eng.activation(atn_b[:, 0:16416],
                           qts[:].rearrange("p a c -> p (a c)"),
                           AF.Square).then_inc(S["ac"], 1)
        qsq_a = inc("ac")
        ncell = 0
        cell_dv = {}
        for hp in range(NHP):
            for sg in range(5):
                w0 = 512 * sg
                sw = 512 if sg < 4 else 1
                bank = spare if ncell % 2 == 0 else bndp
                if on("mm"):
                    if hp == 0 and sg == 0:
                        W("ac", qsq_a)
                    if ncell >= 2:
                        W("dv", cell_dv[ncell - 2])
                    eng.matmul(bank[0:1, 0:sw], onescol[:],
                               atn[:, hp, w0:w0 + sw],
                               start=True, stop=True).then_inc(S["mm"], 1)
                mmn = inc("mm")
                if on("dv"):
                    W("mm", mmn)
                    eng.tensor_reduce(srow2[0:1, ncell:ncell + 1],
                                      bank[0:1, 0:sw], AX.X,
                                      ALU.max).then_inc(S["dv"], 1)
                cell_dv[ncell] = inc("dv")
                ncell += 1
        if on("dv"):
            W("dv", cnt["dv"])
            eng.tensor_reduce(srow[0:1, 1:2], srow2[0:1, 0:ncell], AX.X,
                              ALU.max).then_inc(S["dv"], 1)
        d_q2m = inc("dv")

        if on("dv"):
            W("pl", ph1_done["pl"])
            eng.tensor_reduce(srow[0:1, 0:1], mkac[0:1, 0:64], AX.X,
                              ALU.max).then_inc(S["dv"], 1)
        d_kinf = inc("dv")

        if on("gp"):
            W("ini", 5 * 16)
            eng.tensor_reduce(srow[0:1, 2:3], relb[:], AX.XYZWC, ALU.max,
                              apply_absolute_value=True).then_inc(S["pl"], 1)
        g_mb = inc("pl", 1)
        if on("dv"):
            W("pl", g_mb)
        d_mb = cnt["dv"]

        if SUBPH == 7:
            return finish(me, eng, cnt, S, [])
        if on("ac"):
            W("dv", max(d_q2m, d_kinf, d_mb))
            eng.activation(srow[0:1, 3:4], srow[0:1, 1:2], AF.Ln).then_inc(S["ac"], 1)
        a_ln = inc("ac")
        if on("ac"):
            W("ac", a_ln)
            eng.activation(srow[0:1, 4:5], srow[0:1, 3:4], AF.Exp,
                           scale=0.5).then_inc(S["ac"], 1)
        a_sq = inc("ac")
        if on("dv"):
            W("ac", a_sq)
            eng.tensor_tensor(srow[0:1, 5:6], srow[0:1, 4:5], srow[0:1, 0:1],
                              op=ALU.mult).then_inc(S["dv"], 1)
        d_c1 = inc("dv")
        if on("dv"):
            W("dv", d_c1)
            eng.scalar_tensor_tensor(srow[0:1, 6:7], srow[0:1, 2:3], 2.0,
                                     srow[0:1, 5:6], op0=ALU.mult,
                                     op1=ALU.add).then_inc(S["dv"], 1)
        d_c = inc("dv")
        if on("dv"):
            W("dv", d_c)
            eng.tensor_scalar(srow[0:1, 7:8], srow[0:1, 6:7], -1.0, None,
                              op0=ALU.mult).then_inc(S["dv"], 1)
        d_negc = inc("dv")

        if on("mm"):
            W("dv", d_negc)
            eng.matmul(spare[0:16, 0:32], relbb[:], identb[0:32, 0:32],
                       start=True, stop=True).then_inc(S["mm"], 1)
        t_rt = inc("mm")
        rt_ev = act_copy(stats[0:16, 0:32], spare[0:16, 0:32],
                         wait=[("mm", t_rt)])
        if on("dv"):
            W("ac", rt_ev)
            eng.tensor_tensor(colv[0:16, 2:3], stats[0:16, 31:32],
                              stats[0:16, 0:1], op=ALU.subtract).then_inc(S["dv"], 1)
        d_b31c = inc("dv")
        mmt, rowap = part_row(colv[:, 2:3], 2, d_b31c)
        if on("dv"):
            W("mm", mmt)
            eng.tensor_copy(srow[0:1, 16:32], rowap[0:1, 0:16]).then_inc(S["dv"], 1)
        d_b31r = inc("dv")
        if on("dv"):
            W("dv", d_b31r)
            eng.tensor_tensor(srow[0:1, 32:40], srow[0:1, 17:33:2],
                              srow[0:1, 16:32:2], op=ALU.subtract).then_inc(S["dv"], 1)
        d_d31 = inc("dv")

        if SUBPH == 8:
            return finish(me, eng, cnt, S, [])
        prev_tab_ev = 0
        for hp in range(NHP):
            if on("dv"):
                if hp == 0:
                    W("dv", d_d31)
                    W("dv", d_negc)
                else:
                    W("ac", prev_tab_ev)
                eng.tensor_scalar(srow2[0:1, 0:256], clsm[:],
                                  srow[0:1, 16 + 2 * hp:17 + 2 * hp],
                                  srow[0:1, 7:8], op0=ALU.mult,
                                  op1=ALU.add).then_inc(S["dv"], 1)
            d_row = inc("dv")
            if on("dv"):
                W("dv", d_row)
                eng.tensor_copy(bbldb[0:1, 0:256],
                                srow2[0:1, 0:256]).then_inc(S["dv"], 1)
            d_hi = inc("dv")
            if on("dv"):
                W("dv", d_hi)
                eng.tensor_copy(bbld[0:1, 0:256], bbldb[0:1, 0:256]).then_inc(S["dv"], 1)
            d_hif = inc("dv")
            if on("dv"):
                W("dv", d_hif)
                eng.tensor_tensor(bbld[0:1, 256:512], srow2[0:1, 0:256],
                                  bbld[0:1, 0:256], op=ALU.subtract).then_inc(S["dv"], 1)
            d_lo = inc("dv")
            if on("dv"):
                W("dv", d_lo)
                eng.tensor_copy(bbldb[0:1, 256:512],
                                bbld[0:1, 256:512]).then_inc(S["dv"], 1)
            d_lob = inc("dv")
            for sg in range(2):
                if on("mm"):
                    W("dv", d_lob)
                    if prev_tab_ev:
                        W("ac", prev_tab_ev)
                    eng.matmul(spare[:, 128 * sg:128 * sg + 128], onesb[:],
                               bbldb[0:1, 128 * sg:128 * sg + 128],
                               start=True, stop=False)
                    eng.matmul(spare[:, 128 * sg:128 * sg + 128], onesb[:],
                               bbldb[0:1, 256 + 128 * sg:256 + 128 * sg + 128],
                               start=False, stop=True).then_inc(S["mm"], 1)
                mmb = inc("mm")
                prev_tab_ev = act_copy(
                    vtab[:, 256 * hp + 128 * sg:256 * hp + 128 * sg + 128],
                    spare[:, 128 * sg:128 * sg + 128], wait=[("mm", mmb)])
            if on("dv"):
                W("ac", prev_tab_ev)
                eng.tensor_scalar(srow2[0:1, 0:256], clsm[:],
                                  srow[0:1, 17 + 2 * hp:18 + 2 * hp],
                                  srow[0:1, 7:8], op0=ALU.mult,
                                  op1=ALU.add).then_inc(S["dv"], 1)
            d_row1 = inc("dv")
            if on("dv"):
                W("dv", d_row1)
                eng.tensor_copy(bbldb[0:1, 0:256],
                                srow2[0:1, 0:256]).then_inc(S["dv"], 1)
            d_hi1 = inc("dv")
            if on("dv"):
                W("dv", d_hi1)
                eng.tensor_copy(bbld[0:1, 0:256], bbldb[0:1, 0:256]).then_inc(S["dv"], 1)
            d_hif1 = inc("dv")
            if on("dv"):
                W("dv", d_hif1)
                eng.tensor_tensor(bbld[0:1, 256:512], srow2[0:1, 0:256],
                                  bbld[0:1, 0:256], op=ALU.subtract).then_inc(S["dv"], 1)
            d_lo1 = inc("dv")
            if on("dv"):
                W("dv", d_lo1)
                eng.tensor_copy(bbldb[0:1, 256:512],
                                bbld[0:1, 256:512]).then_inc(S["dv"], 1)
            d_lob1 = inc("dv")
            for sg in range(2):
                if on("mm"):
                    W("dv", d_lob1)
                    W("ac", prev_tab_ev)
                    eng.matmul(spare[:, 256 + 128 * sg:384 + 128 * sg], onesb[:],
                               bbldb[0:1, 128 * sg:128 * sg + 128],
                               start=True, stop=False)
                    eng.matmul(spare[:, 256 + 128 * sg:384 + 128 * sg], onesb[:],
                               bbldb[0:1, 256 + 128 * sg:256 + 128 * sg + 128],
                               start=False, stop=True).then_inc(S["mm"], 1)
                mmb = inc("mm")
                prev_tab_ev = act_copy(
                    vtab1[:, 256 * hp + 128 * sg:256 * hp + 128 * sg + 128],
                    spare[:, 256 + 128 * sg:384 + 128 * sg], wait=[("mm", mmb)])

        if phases < 3:
            keys = (["xlnt", "kt", "vsw", "p8f", "p8r", "qt", "sc", "vtab",
                     "vtab1"] if debug else [])
            return finish(me, eng, cnt, S, keys)

        ktv = pool_flat[:, P2_KT:P2_KT + 16384].rearrange(
            "p (b k) -> p b k", k=8192)
        vbv = pool_flat[:, P2_V:P2_V + 16640].rearrange(
            "p (b k e) -> p b k e", k=64, e=130)
        b8v = pf32[:, 16512:22656].rearrange(
            "p (s h q) -> p s h q", h=2, q=512)
        rbB2 = pf32[:, 22656:23680]
        bbld2 = pf32[0:2, 23680:24704]
        bbldb2 = pool_flat[0:2, 49408:50432]
        stg = pool_flat[0:64, 50432:51458].rearrange("p (h q) -> p h q", q=513)
        rb64h = {
            (0, "hi"): pool_flat[64:65, 51464:51976],
            (0, "lo"): pool_flat[64:65, 51976:52488],
            (1, "hi"): pool_flat[64:65, 52488:53000],
            (1, "lo"): pool_flat[64:65, 53000:53512],
        }
        rbbnd = pool_flat[64:65, 53512:53516]
        rb_f_tmp = pf32[64:65, 26760:27272]
        rb_f_tmp2 = pf32[64:65, 27272:27784]

        act_id = {}
        corr_id = {}
        prev2 = {"b8dv": 0, "pvdone": 0, "stgdma": [], "bndexp": 0,
                 "bnd_norm": 0, "bcast_done": 0, "normP": {0: 0, 1: 0},
                 "bbcast": cnt["dv"], "norm_dv": 0,
                 "kvfree": {0: 0, 1: 0}}
        ph2_ac0 = cnt["ac"]
        ph2_dv0 = cnt["dv"]
        ph2_mm0 = cnt["mm"]

        if on("dv"):
            W("ac", ph2_ac0)
            eng.tensor_tensor(vtab1[:], vtab1[:], vtab[:],
                              op=ALU.subtract).then_inc(S["dv"], 1)
        corr_sub = inc("dv")
        if on("ac"):
            W("dv", corr_sub)
            eng.activation(vtab1[:], vtab1[:], AF.Exp).then_inc(S["ac"], 1)
        corr_tab = inc("ac")
        ph2_ac0 = corr_tab
        ph2_dv0 = corr_sub

        ph2_slot0 = {k: 16 * v for k, v in slot_uses.items() if v > 0}
        ph2_ini0 = cnt.get("ini", 0)
        ph2_sx0 = cnt.get("sx", 0)

        def ph1_drain():
            W("mm", ph2_mm0)
            W("ac", ph2_ac0)
            W("dv", ph2_dv0)
            for k, v in ph2_slot0.items():
                W(k, v)
            if ph2_ini0:
                W("ini", ph2_ini0)
            if ph2_sx0:
                W("sx", ph2_sx0)

        def rt_dma(queue, out, in_ap, off_fn, pre=None):
            name = f"{queue}{dma_rr[queue] % NSLOT}"
            dma_rr[queue] += 1
            if on(queue):
                r = get_rt()
                if slot_uses[name] > 0:
                    W(name, 16 * slot_uses[name])
                if pre is not None:
                    pre()
                srcx = dataclasses.replace(
                    in_ap, offset=in_ap.offset + off_fn(r))
                eng.dma_start(out=out, in_=srcx).then_inc(S[name], 16)
            slot_uses[name] += 1
            cnt[name] = 16 * slot_uses[name]
            return (name, 16 * slot_uses[name])

        def load_kv(hp):
            dbuf = hp % 2
            tl = []
            if on("sy"):
                if hp == 0:
                    ph1_drain()
                W("mm", prev2["kvfree"][dbuf])
            ktw = rt_dma("sy", ktv[:, dbuf, :], ktd[hp, :, 0:8192],
                         lambda r: r["j"] * 2048)
            tl.append(ktw)
            vw = rt_dma(
                "sy", vbv[:, dbuf, :, :],
                vsw[0:64, :, 130 * hp:130 * hp + 130].rearrange("k p c -> p k c"),
                lambda r: r["j"] * (16 * 128 * 1040))
            tl.append(vw)
            return tl

        def load_hp(hp, kv_pre):
            tl = list(kv_pre) if kv_pre is not None else load_kv(hp)
            for h in range(2):
                row = 16384 * (2 * hp + h)
                for c2 in range(4):
                    if on("gp") and h == 0 and c2 == 0:
                        if hp == 0:
                            ph1_drain()
                        W("dv", prev2["b8dv"])
                    srcap = dataclasses.replace(
                        p8f[0:1, 0:512],
                        offset=p8f.offset + row + 7680 + 128 * c2,
                        ap=[[1, 128], [1, 512]])
                    tl.append(dma("gp", b8v[:, c2, h, :], srcap))
                srcap = dataclasses.replace(
                    p8f[0:1, 0:512], offset=p8f.offset + row + 7552,
                    ap=[[1, 128], [1, 512]])
                tl.append(dma("gp", b8v[:, 5, h, :], srcap))
                base4 = dataclasses.replace(
                    p8f[0:1, 0:512], offset=p8f.offset + row + 7552,
                    ap=[[1, 128], [1, 512]])
                tl.append(rt_dma("gp", b8v[:, 4, h, :], base4,
                                 lambda r: (1 - (r["j"] >= 1)) * 8192))
            return tl

        def build_bb(hp):
            lastmm = 0
            cb = prev2.get("bb_lastcb", 0)
            for sblk in range(8):
                if on("gp"):
                    W("dv", cb)
                    if sblk == 0:
                        W("ac", prev2.get("bnd_sp_ac", 0))
                w = dma("gp", bbld2[0:2, 0:1024],
                        p8r[2 * hp:2 * hp + 2, 1024 * sblk:1024 * sblk + 1024])
                if on("dv"):
                    W(w)
                    W("mm", lastmm)
                    eng.tensor_copy(bbldb2[0:2, 0:1024],
                                    bbld2[0:2, 0:1024]).then_inc(S["dv"], 1)
                cb = inc("dv")
                if on("mm"):
                    W("dv", cb)
                    if sblk == 0:
                        W("dv", prev2["bbcast"])
                        W("dv", prev2["norm_dv"])
                        W("ac", prev2.get("bnd_sp_ac", 0))
                    for b in range(8):
                        ins = eng.matmul(
                            spare[:, 16 * sblk + 2 * b:16 * sblk + 2 * b + 2],
                            bbldb2[0:2, 128 * b:128 * b + 128],
                            identb[0:2, 0:2], start=True, stop=True)
                        if b == 7:
                            ins.then_inc(S["mm"], 1)
                lastmm = inc("mm")
            if on("dv"):
                W("mm", lastmm)
                eng.tensor_copy(bb_st[:], spare[:, 0:128]).then_inc(S["dv"], 1)
            prev2["bbcast"] = inc("dv")
            prev2["bb_lastcb"] = cb
            return prev2["bbcast"]

        kv_pending = None
        for hp in range(min(NHP, PH2N)):
            dbuf = hp % 2
            kvw = load_hp(hp, kv_pending)
            kv_pending = None
            if PH2M < 1:
                continue
            bbdv = build_bb(hp)
            if PH2M < 2:
                continue
            for qt in range(NQT):
                qcol0 = 1 + 512 * qt
                DEP = 2
                pvp = ((pv0, pv1) if qt % 2 == 0
                       else (bndp[0:65, 0:512], spare[0:65, 0:512]))
                for c2 in range(NCH):
                    cstar = (c2 + 4 * qt) % 64
                    pair = ((s_a[:, 0, :], s_a[:, 1, :]),
                            (s_b[:, 0, :], s_b[:, 1, :]))[c2 % 2]
                    near = c2 in NEAR
                    b8slot = c2 if c2 < 4 else (4 if qt == 0 else 5)
                    colidx = hp * 256 + qt * 64 + c2
                    def bnd_pair(cst_):
                        eng.matmul(bndp[:, c2:c2 + 1],
                                   ktv[0:64, dbuf, 128 * cst_:128 * cst_ + 128],
                                   qts[0:64, hp, 0:1], start=True, stop=True)
                        eng.matmul(spare[:, c2:c2 + 1],
                                   ktv[64:128, dbuf,
                                       128 * cst_:128 * cst_ + 128],
                                   qts[64:128, hp, 0:1], start=True,
                                   stop=True, tile_position=(64, 0))
                    if on("mm"):
                        if c2 >= DEP:
                            W("ac", act_id[(hp, qt, c2 - DEP)])
                        elif qt > 0:
                            W("ac", act_id[(hp, qt - 1, NCH - DEP + c2)])
                            if c2 < 2:
                                W("dv", prev2["bcast_done"])
                        elif hp > 0:
                            W("ac", act_id[(hp - 1, NQT - 1, NCH - DEP + c2)])
                            if c2 < 2:
                                W("dv", prev2["bcast_done"])
                        else:
                            W("ac", ph2_ac0)
                            W("dv", ph2_dv0)
                        if c2 == 0 and qt == 0:
                            for t_ in kvw[:2]:
                                W(t_)
                        if qt == 0 and c2 == 0 and (PH2X & 2):
                            W("dv", bbdv)
                            bnd_pair(cstar)
                        eng.matmul(pair[0],
                                   ktv[0:64, dbuf, 128 * cstar:128 * cstar + 128],
                                   qts[0:64, hp, qcol0:qcol0 + 512],
                                   start=True, stop=True)
                        eng.matmul(pair[1],
                                   ktv[64:128, dbuf, 128 * cstar:128 * cstar + 128],
                                   qts[64:128, hp, qcol0:qcol0 + 512],
                                   start=True, stop=True,
                                   tile_position=(64, 0)).then_inc(S["mm"], 1)
                    s_inc = inc("mm")
                    if near and not (PH2X & 1):
                        near = False
                    if near:
                        if on("dv"):
                            W("mm", s_inc)
                            if qt == 0 and c2 == 0:
                                for t_ in kvw[2:]:
                                    W(t_)
                            def b8rev(sl, hh):
                                bb_ = b8v[:, sl, hh, :]
                                return dataclasses.replace(
                                    bb_, ap=[[bb_.ap[0][0], 128], [-1, 512]],
                                    offset=bb_.offset + 511)
                            eng.tensor_tensor(pair[0], pair[0],
                                              b8rev(b8slot, 0), op=ALU.add)
                            eng.tensor_tensor(pair[1], pair[1],
                                              b8rev(b8slot, 1),
                                              op=ALU.add).then_inc(S["dv"], 1)
                        b8dv = inc("dv")
                        if qt == NQT - 1 and c2 == NCH - 1:
                            prev2["b8dv"] = b8dv
                    else:
                        b8dv = None
                    buf_t = s_a if c2 % 2 == 0 else s_b
                    if on("ac"):
                        W("mm", s_inc)
                        if b8dv is not None:
                            W("dv", b8dv)
                        eng.activation(ptb[:, c2 % 2, :, :], buf_t[:],
                                       AF.Exp, bias=vtab[:, colidx:colidx + 1],
                                       scale=0.125).then_inc(S["ac"], 1)
                    act_id[(hp, qt, c2)] = inc("ac")
                    if on("dv"):
                        W("ac", act_id[(hp, qt, c2)])
                        eng.tensor_scalar(ptb[:, c2 % 2, 1, :],
                                          ptb[:, c2 % 2, 1, :],
                                          vtab1[:, colidx:colidx + 1], None,
                                          op0=ALU.mult).then_inc(S["dv"], 1)
                    corr_id[(hp, qt, c2)] = inc("dv")
                    if c2 >= 1 and (PH2X & 4):
                        cprev = c2 - 1
                        cstar_p = (cprev + 4 * qt) % 64
                        if on("mm"):
                            W("ac", act_id[(hp, qt, cprev)])
                            W("dv", corr_id[(hp, qt, cprev)])
                            if cprev == 0:
                                W("dv", prev2["normP"][qt % 2])
                                if qt == 1:
                                    W("ac", prev2["bndexp"])
                                    W("dv", prev2["bnd_norm"])
                            eng.matmul(pvp[0][:], vbv[:, dbuf, cstar_p, 0:65],
                                       ptb[:, cprev % 2, 0, :],
                                       start=(cprev == 0), stop=False)
                            eng.matmul(pvp[1][:], vbv[:, dbuf, cstar_p, 65:130],
                                       ptb[:, cprev % 2, 1, :],
                                       start=(cprev == 0), stop=False)
                            if qt == 0 and (PH2X & 2):
                                bnd_pair(cstar)
                    elif c2 >= 1 and qt == 0 and (PH2X & 2):
                        if on("mm"):
                            bnd_pair(cstar)
                cprev = NCH - 1
                cstar_p = (cprev + 4 * qt) % 64
                if on("mm") and (PH2X & 4):
                    W("ac", act_id[(hp, qt, cprev)])
                    W("dv", corr_id[(hp, qt, cprev)])
                    eng.matmul(pvp[0][:], vbv[:, dbuf, cstar_p, 0:65],
                               ptb[:, cprev % 2, 0, :], start=False, stop=True)
                    eng.matmul(pvp[1][:], vbv[:, dbuf, cstar_p, 65:130],
                               ptb[:, cprev % 2, 1, :], start=False,
                               stop=True).then_inc(S["mm"], 1)
                pv_done = inc("mm") if (PH2X & 4) else cnt["mm"]
                if qt == NQT - 1:
                    prev2["kvfree"][dbuf] = pv_done
                    prev2["bnd_sp_ac"] = act_id[(hp, qt, NCH - 1)]

                if PH2M < 3:
                    continue
                bnd_norm = 0
                if qt == 0:
                    bbsl = bb_st[:, 0:64]
                    ebb0 = dataclasses.replace(
                        bbsl, ap=[[bbsl.ap[0][0], 128], [2, 64]])
                    ebb1 = dataclasses.replace(
                        bbsl, ap=[[bbsl.ap[0][0], 128], [2, 64]],
                        offset=bbsl.offset + 1)
                    if on("dv"):
                        W("mm", pv_done)
                        W("dv", bbdv)
                        eng.tensor_tensor(bndp[:, 0:64], bndp[:, 0:64],
                                          ebb0, op=ALU.add)
                        eng.tensor_tensor(spare[:, 0:64], spare[:, 0:64],
                                          ebb1, op=ALU.add).then_inc(S["dv"], 1)
                    bbadd = inc("dv")
                    if on("ac"):
                        W("dv", bbadd)
                        eng.activation(ptbd[:, 0:64], bndp[:, 0:64], AF.Exp,
                                       bias=vtab[:, 256 * hp:256 * hp + 1],
                                       scale=0.125)
                        eng.activation(ptbd[:, 64:128], spare[:, 0:64], AF.Exp,
                                       bias=vtab[:, 256 * hp:256 * hp + 1],
                                       scale=0.125).then_inc(S["ac"], 1)
                    bndexp = inc("ac")
                    prev2["bnd_sp_ac"] = bndexp
                    prev2["bndexp"] = bndexp
                    if on("mm"):
                        W("ac", bndexp)
                        for h in range(2):
                            for c2b in range(NCH):
                                cst = c2b % 64
                                ins = eng.matmul(
                                    bndp[0:65, 128 + h:129 + h],
                                    vbv[:, dbuf, cst, 65 * h:65 * h + 65],
                                    ptbd[:, 64 * h + c2b:64 * h + c2b + 1],
                                    start=(c2b == 0), stop=(c2b == NCH - 1))
                        ins.then_inc(S["mm"], 1)
                    bndpv = inc("mm")
                    if on("dv"):
                        W("mm", bndpv)
                        for t_ in prev2["stgdma"]:
                            W(t_)
                        eng.reciprocal(rbB2[64:65, 1020:1022],
                                       bndp[64:65, 128:130]).then_inc(S["dv"], 1)
                    r1 = inc("dv")
                    if on("dv"):
                        W("dv", r1)
                        eng.tensor_copy(rbbnd[0:1, 0:2], rbB2[64:65, 1020:1022]).then_inc(S["dv"], 1)
                    r2 = inc("dv")
                    if on("dv"):
                        W("dv", r2)
                        eng.tensor_copy(rb_f_tmp[0:1, 0:2], rbbnd[0:1, 0:2]).then_inc(S["dv"], 1)
                    r2b = inc("dv")
                    if on("dv"):
                        W("dv", r2b)
                        eng.tensor_tensor(rb_f_tmp2[0:1, 0:2], rbB2[64:65, 1020:1022],
                                          rb_f_tmp[0:1, 0:2],
                                          op=ALU.subtract).then_inc(S["dv"], 1)
                    r3 = inc("dv")
                    if on("dv"):
                        W("dv", r3)
                        eng.tensor_copy(rbbnd[0:1, 2:4],
                                        rb_f_tmp2[0:1, 0:2]).then_inc(S["dv"], 1)
                    r4 = inc("dv")
                    if on("mm"):
                        W("dv", r4)
                        W("ac", act_id[(hp, 0, NCH - 2)])
                        eng.matmul(s_a[0:64, 0, 0:2], onesw[64:65, 0:64],
                                   rbbnd[0:1, 0:2], start=True, stop=False)
                        eng.matmul(s_a[0:64, 0, 0:2], onesw[64:65, 0:64],
                                   rbbnd[0:1, 2:4], start=False,
                                   stop=True).then_inc(S["mm"], 1)
                    bb_bc = inc("mm")
                    if on("dv"):
                        W("mm", bb_bc)
                        eng.tensor_copy(rbB2[0:64, 1022:1024],
                                        s_a[0:64, 0, 0:2]).then_inc(S["dv"], 1)
                    bb_bc2 = inc("dv")
                    for h in range(2):
                        if on("dv"):
                            W("dv", bb_bc2)
                            ins = eng.tensor_tensor(stg[:, h, 0:1],
                                                    bndp[0:64, 128 + h:129 + h],
                                                    rbB2[0:64, 1022 + h:1023 + h],
                                                    op=ALU.mult)
                            if h == 1:
                                ins.then_inc(S["dv"], 1)
                    bnd_norm = inc("dv")
                    prev2["bnd_norm"] = bnd_norm

                if PH2M < 4:
                    continue
                if on("dv"):
                    W("mm", pv_done)
                    for t_ in prev2["stgdma"]:
                        W(t_)
                    eng.reciprocal(rbB2[64:65, 0:512], pvp[0][64:65, :])
                    eng.reciprocal(rbB2[64:65, 512:1024],
                                   pvp[1][64:65, :]).then_inc(S["dv"], 1)
                n1 = inc("dv")
                if on("dv"):
                    W("dv", n1)
                    eng.tensor_copy(rb64h[(0, "hi")], rbB2[64:65, 0:512])
                    eng.tensor_copy(rb64h[(1, "hi")],
                                    rbB2[64:65, 512:1024]).then_inc(S["dv"], 1)
                n2 = inc("dv")
                if on("dv"):
                    W("dv", n2)
                    eng.tensor_copy(rb_f_tmp[0:1, 0:512], rb64h[(0, "hi")])
                    eng.tensor_copy(rb_f_tmp2[0:1, 0:512],
                                    rb64h[(1, "hi")]).then_inc(S["dv"], 1)
                n3 = inc("dv")
                if on("dv"):
                    W("dv", n3)
                    eng.tensor_tensor(rb_f_tmp[0:1, 0:512], rbB2[64:65, 0:512],
                                      rb_f_tmp[0:1, 0:512], op=ALU.subtract)
                    eng.tensor_tensor(rb_f_tmp2[0:1, 0:512], rbB2[64:65, 512:1024],
                                      rb_f_tmp2[0:1, 0:512],
                                      op=ALU.subtract).then_inc(S["dv"], 1)
                n4 = inc("dv")
                if on("dv"):
                    W("dv", n4)
                    eng.tensor_copy(rb64h[(0, "lo")], rb_f_tmp[0:1, 0:512])
                    eng.tensor_copy(rb64h[(1, "lo")],
                                    rb_f_tmp2[0:1, 0:512]).then_inc(S["dv"], 1)
                n5 = inc("dv")
                if on("mm"):
                    W("dv", n5)
                    W("ac", act_id[(hp, qt, NCH - 1)])
                    eng.matmul(s_a[0:64, 0, :], onesw[64:65, 0:64],
                               rb64h[(0, "hi")], start=True, stop=False)
                    eng.matmul(s_a[0:64, 0, :], onesw[64:65, 0:64],
                               rb64h[(0, "lo")], start=False, stop=True)
                    eng.matmul(s_b[0:64, 0, :], onesw[64:65, 0:64],
                               rb64h[(1, "hi")], start=True, stop=False)
                    eng.matmul(s_b[0:64, 0, :], onesw[64:65, 0:64],
                               rb64h[(1, "lo")], start=False,
                               stop=True).then_inc(S["mm"], 1)
                n_bc = inc("mm")
                if on("dv"):
                    W("mm", n_bc)
                    eng.tensor_copy(rbB2[0:64, 0:512], s_a[0:64, 0, :])
                    eng.tensor_copy(rbB2[0:64, 512:1024],
                                    s_b[0:64, 0, :]).then_inc(S["dv"], 1)
                n_bc2 = inc("dv")
                prev2["bcast_done"] = n_bc2
                for h in range(2):
                    pvh = pvp[h]
                    if on("dv"):
                        W("dv", n_bc2)
                        ins = eng.tensor_tensor(
                            stg[:, h, 1:513], pvh[0:64, :],
                            rbB2[0:64, 512 * h:512 * h + 512], op=ALU.mult)
                        if h == 1:
                            ins.then_inc(S["dv"], 1)
                norm_dv = inc("dv")
                prev2["norm_dv"] = norm_dv
                prev2["normP"][qt % 2] = norm_dv
                c0 = 0 if qt == 0 else 1
                dmas = []
                for h in range(2):
                    if on("gp"):
                        W("dv", norm_dv)
                        if qt == 0:
                            W("dv", bnd_norm)
                    dmas.append(dma("gp",
                                    atn[64 * h:64 * h + 64, hp,
                                        qcol0 - 1 + c0:qcol0 + 512],
                                    stg[:, h, c0:513]))
                prev2["stgdma"] = dmas
                if qt == 0 and hp + 1 < min(NHP, PH2N):
                    kv_pending = load_kv(hp + 1)

        ph2_done = dict(cnt)
        if phases < 4:
            keys = (["atn", "sc", "kt", "vsw", "qt", "p8f", "vtab", "vtab1"]
                    if debug else [])
            return finish(me, eng, cnt, S, keys)

        ph2_fin = {"mm": cnt["mm"], "ac": cnt["ac"], "dv": cnt["dv"]}
        ph2_slots = {k: 16 * v for k, v in slot_uses.items() if v > 0}

        def ph2_drain():
            W("mm", ph2_fin["mm"])
            W("ac", ph2_fin["ac"])
            W("dv", ph2_fin["dv"])
            for k_, v_ in ph2_slots.items():
                W(k_, v_)

        w3v = pool_flat[:, 0:32768].rearrange("p (w cc d) -> p w cc d",
                                              cc=8, d=1024)
        rfv = pf32[:, 16384:20480].rearrange("p (dd c) -> p dd c", c=512)
        rff = pf32[:, 16384:20480]
        xrv = pool_flat[:, P3_XR:P3_XR + 4096].rearrange(
            "p (cc c) -> p cc c", c=512)
        xkv = pool_flat[:, P3_XK:P3_XK + 4096].rearrange(
            "p (cc c) -> p cc c", c=512)
        tm4 = pf32[:, 24576:28672].rearrange("p (a c) -> p a c", c=1024)
        tm8 = pf32[:, 24576:28672].rearrange("p (a c) -> p a c", c=512)
        ptbF = ptb[:].rearrange("p a b c -> p (a b c)")
        atnF = atn[:].rearrange("p a c -> p (a c)").bitcast(F32)
        x2rd = atnF[:, 0:4096]
        banksl = [s_a[:, 0, :], s_a[:, 1, :], s_b[:, 0, :], s_b[:, 1, :]]

        w3l = []
        for wi in range(4):
            if on("gp") and wi == 0:
                ph2_drain()
            w3l.append(dma("gp", pool_flat[:, 8192 * wi:8192 * wi + 8192],
                           w3d[wi, :, :, :].rearrange("p cc d -> p (cc d)")))

        p3 = {"opev": {0: 0, 1: 0, 2: 0, 3: 0}, "btev": 0,
              "xfree": {s: (0, 0, None) for s in range(4)},
              "hilo": 0, "tp_last": 0, "bankrd": 0, "x2slot": None,
              "ln2_last": 0, "acc2_last": 0}

        FEN = {"dump": 0}

        def ln_psum(sc0, tp_wait):
            if on("ac"):
                W("mm", tp_wait)
                W("ac", FEN["dump"])
                eng.activation(ptbF[:, 1024:1536], bndp[:, 0:512], AF.Identity,
                               accum_out=sums[:, 0, 0:1])
                eng.activation(ptbF[:, 1536:2048], spare[:, 0:512], AF.Identity,
                               accum_out=sums[:, 1, 0:1])
                eng.activation(ptbF[:, 0:512], bndp[:, 0:512], AF.Square,
                               accum_out=sums[:, 0, 1:2])
                eng.activation(ptbF[:, 512:1024], spare[:, 0:512], AF.Square,
                               accum_out=sums[:, 1, 1:2]).then_inc(S["ac"], 1)
            acc = inc("ac")
            FEN["dump"] = acc
            if on("dv"):
                W("ac", acc)
                eng.tensor_tensor(stats[:, sc0:sc0 + 2], sums[:, 0, 0:2],
                                  sums[:, 1, 0:2], op=ALU.add).then_inc(S["dv"], 1)
            d1 = inc("dv")
            if on("dv"):
                W("dv", d1)
                eng.tensor_scalar(stats[:, sc0 + 2:sc0 + 3],
                                  stats[:, sc0:sc0 + 1], -1.0 / 1024, None,
                                  op0=ALU.mult)
                eng.tensor_scalar(stats[:, sc0 + 3:sc0 + 4],
                                  stats[:, sc0 + 1:sc0 + 2], 1.0 / 1024, None,
                                  op0=ALU.mult).then_inc(S["dv"], 1)
            d2 = inc("dv")
            if on("dv"):
                W("dv", d2)
                eng.tensor_tensor(stats[:, sc0 + 4:sc0 + 5],
                                  stats[:, sc0 + 2:sc0 + 3],
                                  stats[:, sc0 + 2:sc0 + 3],
                                  op=ALU.mult).then_inc(S["dv"], 1)
            d3 = inc("dv")
            if on("dv"):
                W("dv", d3)
                eng.scalar_tensor_tensor(stats[:, sc0 + 3:sc0 + 4],
                                         stats[:, sc0 + 3:sc0 + 4], 1e-5,
                                         stats[:, sc0 + 4:sc0 + 5],
                                         op0=ALU.add,
                                         op1=ALU.subtract).then_inc(S["dv"], 1)
            d4 = inc("dv")
            if on("ac"):
                W("dv", d4)
                eng.activation(stats[:, sc0 + 5:sc0 + 6],
                               stats[:, sc0 + 3:sc0 + 4], AF.Ln).then_inc(S["ac"], 1)
            a1 = inc("ac")
            if on("ac"):
                W("ac", a1)
                eng.activation(stats[:, sc0 + 6:sc0 + 7],
                               stats[:, sc0 + 5:sc0 + 6], AF.Exp,
                               scale=-0.5).then_inc(S["ac"], 1)
            a2 = inc("ac")
            if on("dv"):
                W("ac", a2)
                eng.tensor_tensor(stats[:, sc0 + 7:sc0 + 8],
                                  stats[:, sc0 + 2:sc0 + 3],
                                  stats[:, sc0 + 6:sc0 + 7],
                                  op=ALU.mult).then_inc(S["dv"], 1)
            d5 = inc("dv")
            if on("dv"):
                W("dv", d5)
                eng.tensor_scalar(bndp[:, 0:512], bndp[:, 0:512],
                                  stats[:, sc0 + 6:sc0 + 7],
                                  stats[:, sc0 + 7:sc0 + 8], op0=ALU.mult,
                                  op1=ALU.add)
                eng.tensor_scalar(spare[:, 0:512], spare[:, 0:512],
                                  stats[:, sc0 + 6:sc0 + 7],
                                  stats[:, sc0 + 7:sc0 + 8], op0=ALU.mult,
                                  op1=ALU.add).then_inc(S["dv"], 1)
            return inc("dv")

        for t3 in range(4):
            C0 = 512 * t3
            for dd in range(8):
                bank = banksl[dd % 4]
                if on("mm"):
                    if t3 == 0 and dd == 0:
                        ph2_drain()
                        for t_ in w3l[:1]:
                            W(t_)
                    if dd >= 4:
                        W("ac", p3["opev"][dd % 4])
                    elif t3 > 0 or dd >= 4:
                        W("ac", p3["opev"][dd % 4])
                    if dd == 0:
                        W("ac", p3["btev"])
                    ins = None
                    for cc in range(8):
                        ins = eng.matmul(bank, w3v[:, 0, cc, 128 * dd:128 * dd + 128],
                                         atn[:, cc, C0:C0 + 512],
                                         start=(cc == 0), stop=(cc == 7))
                    ins.then_inc(S["mm"], 1)
                mmid = inc("mm")
                if on("ac"):
                    W("mm", mmid)
                    if t3 > 0 and dd == 0:
                        W("dv", p3["hilo"])
                    eng.activation(rfv[:, dd, :], bank, AF.Identity,
                                   bias=vecs[:, 24 + dd:25 + dd]).then_inc(S["ac"], 1)
                p3["opev"][dd % 4] = inc("ac")
            opev7 = p3["opev"][3]
            if on("dv"):
                for b_ in range(4):
                    W("ac", p3["opev"][b_])
                if t3 > 0:
                    W("mm", p3["tp_last"])
                eng.tensor_copy(xrv[:, :, :], rfv[:, :, :]).then_inc(S["dv"], 1)
            hi_a = inc("dv")
            if on("dv"):
                W("dv", hi_a)
                eng.scalar_tensor_tensor(xkv[:, :, :], xrv[:, :, :], -1.0,
                                         rfv[:, :, :], op0=ALU.mult,
                                         op1=ALU.add).then_inc(S["dv"], 1)
            p3["hilo"] = inc("dv")

            for s3 in range(4):
                ac_f, dv_f, sl_f = p3["xfree"][s3]
                if on("gp"):
                    W("ac", ac_f)
                    W("dv", dv_f)
                    if sl_f is not None:
                        W(sl_f)
                wx = dma("gp", tm4[:, s3, :],
                         xsd[C0 + 128 * s3:C0 + 128 * s3 + 128, :])
                if on("mm"):
                    W("dv", p3["hilo"])
                    W("dv", p3["bankrd"])
                    ins = None
                    for dd in range(8):
                        bank = bndp if dd < 4 else spare
                        col = 128 * (dd % 4)
                        eng.matmul(bank[:, col:col + 128],
                                   xrv[:, dd, 128 * s3:128 * s3 + 128],
                                   identb[:], start=True, stop=False)
                        ins = eng.matmul(bank[:, col:col + 128],
                                         xkv[:, dd, 128 * s3:128 * s3 + 128],
                                         identb[:], start=False, stop=True)
                    ins.then_inc(S["mm"], 1)
                tp = inc("mm")
                p3["tp_last"] = tp
                lnp = ln_psum(16, tp)
                if on("dv"):
                    W("dv", lnp)
                    W(wx)
                    eng.tensor_tensor(tm4[:, s3, 0:512], bndp[:, 0:512],
                                      tm4[:, s3, 0:512], op=ALU.add)
                    eng.tensor_tensor(tm4[:, s3, 512:1024], spare[:, 0:512],
                                      tm4[:, s3, 512:1024],
                                      op=ALU.add).then_inc(S["dv"], 1)
                x2id = inc("dv")
                p3["bankrd"] = x2id
                if on("gp"):
                    W("dv", x2id)
                wx2 = dma("gp", x2d[C0 + 128 * s3:C0 + 128 * s3 + 128, :],
                          tm4[:, s3, :])
                p3["x2slot"] = wx2
                if on("ac"):
                    W("dv", x2id)
                    eng.activation(ptbF[:, 1024:2048], tm4[:, s3, :], AF.Identity,
                                   accum_out=sums[:, 2, 0:1])
                    eng.activation(ptbF[:, 0:1024], tm4[:, s3, :], AF.Square,
                                   accum_out=sums[:, 2, 1:2]).then_inc(S["ac"], 1)
                acc2 = inc("ac")
                FEN["dump"] = acc2
                if on("dv"):
                    W("ac", acc2)
                    eng.tensor_scalar(stats[:, 24:25], sums[:, 2, 0:1],
                                      -1.0 / 1024, None, op0=ALU.mult)
                    eng.tensor_scalar(stats[:, 25:26], sums[:, 2, 1:2],
                                      1.0 / 1024, None, op0=ALU.mult).then_inc(S["dv"], 1)
                e1 = inc("dv")
                if on("dv"):
                    W("dv", e1)
                    eng.tensor_tensor(stats[:, 26:27], stats[:, 24:25],
                                      stats[:, 24:25], op=ALU.mult).then_inc(S["dv"], 1)
                e2 = inc("dv")
                if on("dv"):
                    W("dv", e2)
                    eng.scalar_tensor_tensor(stats[:, 25:26], stats[:, 25:26],
                                             1e-5, stats[:, 26:27], op0=ALU.add,
                                             op1=ALU.subtract).then_inc(S["dv"], 1)
                e3 = inc("dv")
                if on("ac"):
                    W("dv", e3)
                    eng.activation(stats[:, 27:28], stats[:, 25:26],
                                   AF.Ln).then_inc(S["ac"], 1)
                e4 = inc("ac")
                if on("ac"):
                    W("ac", e4)
                    eng.activation(stats[:, 28:29], stats[:, 27:28], AF.Exp,
                                   scale=-0.5).then_inc(S["ac"], 1)
                e5 = inc("ac")
                if on("dv"):
                    W("ac", e5)
                    W("mm", p3.get("bt_mm", 0))
                    eng.tensor_scalar(ptbF[:, 0:1024], tm4[:, s3, :],
                                      stats[:, 24:25], stats[:, 28:29],
                                      op0=ALU.add, op1=ALU.mult).then_inc(S["dv"], 1)
                ln2 = inc("dv")
                p3["ln2_last"] = ln2
                p3["acc2_last"] = acc2
                p3["xfree"][s3] = (acc2, ln2, wx2)
                if on("mm"):
                    W("dv", ln2)
                    W("ac", p3["btev"])
                    ins = None
                    for dd in range(8):
                        bank = s_a[:, 0, :] if dd < 4 else s_a[:, 1, :]
                        col = 128 * (dd % 4)
                        ins = eng.matmul(bank[:, col:col + 128],
                                         ptbF[:, 128 * dd:128 * dd + 128],
                                         identb[:], start=True, stop=True)
                    ins.then_inc(S["mm"], 1)
                bt = inc("mm")
                p3["bt_mm"] = bt
                if on("ac"):
                    W("mm", bt)
                    eng.activation(
                        qts[:, 0:4, C0 + 128 * s3:C0 + 128 * s3 + 128],
                        s_a[:, 0, :].rearrange("p (cc c) -> p cc c", c=128),
                        AF.Identity)
                    eng.activation(
                        qts[:, 4:8, C0 + 128 * s3:C0 + 128 * s3 + 128],
                        s_a[:, 1, :].rearrange("p (cc c) -> p cc c", c=128),
                        AF.Identity).then_inc(S["ac"], 1)
                p3["btev"] = inc("ac")

        def ln_col8(src8, wait_key, wait_val, dbg_tap=False):
            if on("dv"):
                W(wait_key, wait_val)
                eng.tensor_copy(ptbF[:, 1024:1032], src8).then_inc(S["dv"], 1)
            h1a = inc("dv")
            if on("dv"):
                W("dv", h1a)
                eng.scalar_tensor_tensor(ptbF[:, 1032:1040], ptbF[:, 1024:1032],
                                         -1.0, src8, op0=ALU.mult,
                                         op1=ALU.add).then_inc(S["dv"], 1)
            h1 = inc("dv")
            if on("ac"):
                W("dv", h1)
                eng.activation(stats[:, 40:48], src8, AF.Square).then_inc(S["ac"], 1)
            sq = inc("ac")
            if on("dv"):
                W("ac", sq)
                eng.tensor_copy(ptbF[:, 1040:1048],
                                stats[:, 40:48]).then_inc(S["dv"], 1)
            h2a = inc("dv")
            if on("dv"):
                W("dv", h2a)
                eng.scalar_tensor_tensor(ptbF[:, 1048:1056], ptbF[:, 1040:1048],
                                         -1.0, stats[:, 40:48], op0=ALU.mult,
                                         op1=ALU.add).then_inc(S["dv"], 1)
            h2 = inc("dv")
            if on("mm"):
                W("dv", h2)
                eng.matmul(spare[0:1, 0:8], onescol[:], ptbF[:, 1024:1032],
                           start=True, stop=False)
                eng.matmul(spare[0:1, 0:8], onescol[:], ptbF[:, 1032:1040],
                           start=False, stop=True)
                eng.matmul(spare[0:1, 8:16], onescol[:], ptbF[:, 1040:1048],
                           start=True, stop=False)
                eng.matmul(spare[0:1, 8:16], onescol[:], ptbF[:, 1048:1056],
                           start=False, stop=True).then_inc(S["mm"], 1)
            sm = inc("mm")
            if dbg_tap:
                if on("dv"):
                    W("dv", h2)
                    eng.tensor_copy(c8dbg[:, 32:40], ptbF[:, 1024:1032])
                    eng.tensor_copy(c8dbg[:, 40:48], ptbF[:, 1032:1040])
                    eng.tensor_copy(c8dbg[:, 48:56], ptbF[:, 1040:1048])
                    eng.tensor_copy(c8dbg[:, 56:64],
                                    ptbF[:, 1048:1056]).then_inc(S["dv"], 1)
                inc("dv")
                tp_ev = act_copy(c8dbg[0:1, 64:80], spare[0:1, 0:16],
                                 wait=[("mm", sm)])
            ev = act_copy(srow2[0:1, 0:16], spare[0:1, 0:16], wait=[("mm", sm)])
            if dbg_tap:
                if on("dv"):
                    W("ac", ev)
                    eng.tensor_copy(c8dbg[0:1, 80:96],
                                    srow2[0:1, 0:16]).then_inc(S["dv"], 1)
                inc("dv")
            if on("dv"):
                W("ac", ev)
                eng.tensor_reduce(srow2[0:1, 16:17], srow2[0:1, 0:8],
                                  AX.X, ALU.add)
                eng.tensor_reduce(srow2[0:1, 17:18], srow2[0:1, 8:16],
                                  AX.X, ALU.add).then_inc(S["dv"], 1)
            g1 = inc("dv")
            if on("dv"):
                W("dv", g1)
                eng.tensor_scalar(srow2[0:1, 18:19], srow2[0:1, 16:17],
                                  -1.0 / 1024, None, op0=ALU.mult)
                eng.tensor_scalar(srow2[0:1, 19:20], srow2[0:1, 17:18],
                                  1.0 / 1024, None, op0=ALU.mult).then_inc(S["dv"], 1)
            g2 = inc("dv")
            if dbg_tap:
                if on("dv"):
                    W("dv", g2)
                    eng.tensor_copy(c8dbg[0:1, 96:104],
                                    srow2[0:1, 16:24]).then_inc(S["dv"], 1)
                g2 = inc("dv")
            if on("dv"):
                W("dv", g2)
                eng.tensor_tensor(srow2[0:1, 20:21], srow2[0:1, 18:19],
                                  srow2[0:1, 18:19], op=ALU.mult).then_inc(S["dv"], 1)
            g3 = inc("dv")
            if on("dv"):
                W("dv", g3)
                eng.scalar_tensor_tensor(srow2[0:1, 19:20], srow2[0:1, 19:20],
                                         1e-5, srow2[0:1, 20:21], op0=ALU.add,
                                         op1=ALU.subtract).then_inc(S["dv"], 1)
            g4 = inc("dv")
            if dbg_tap:
                if on("dv"):
                    W("dv", g4)
                    eng.tensor_copy(c8dbg[0:1, 104:112],
                                    srow2[0:1, 16:24]).then_inc(S["dv"], 1)
                g4 = inc("dv")
            if on("ac"):
                W("dv", g4)
                eng.activation(srow2[0:1, 21:22], srow2[0:1, 19:20],
                               AF.Ln).then_inc(S["ac"], 1)
            g5a = inc("ac")
            if on("ac"):
                W("ac", g5a)
                eng.activation(srow2[0:1, 22:23], srow2[0:1, 21:22], AF.Exp,
                               scale=-0.5).then_inc(S["ac"], 1)
            g5 = inc("ac")
            if on("dv"):
                W("ac", g5)
                eng.tensor_tensor(srow2[0:1, 23:24], srow2[0:1, 18:19],
                                  srow2[0:1, 22:23], op=ALU.mult).then_inc(S["dv"], 1)
            g6 = inc("dv")
            if dbg_tap:
                if on("dv"):
                    W("dv", g6)
                    eng.tensor_copy(c8dbg[0:1, 112:120],
                                    srow2[0:1, 16:24]).then_inc(S["dv"], 1)
                g6 = inc("dv")
            if on("dv"):
                W("dv", g6)
                eng.tensor_copy(colvb[0:1, 0:2],
                                srow2[0:1, 22:24]).then_inc(S["dv"], 1)
            g7a = inc("dv")
            if dbg_tap:
                if on("dv"):
                    W("dv", g7a)
                    eng.tensor_copy(c8dbg[0:1, 26:28],
                                    colvb[0:1, 0:2]).then_inc(S["dv"], 1)
                g7a = inc("dv")
            if on("dv"):
                W("dv", g7a)
                eng.tensor_copy(srow[0:1, 64:66],
                                colvb[0:1, 0:2]).then_inc(S["dv"], 1)
            g7 = inc("dv")
            if on("dv"):
                W("dv", g7)
                eng.tensor_tensor(srow[0:1, 66:68], srow2[0:1, 22:24],
                                  srow[0:1, 64:66], op=ALU.subtract).then_inc(S["dv"], 1)
            g8 = inc("dv")
            if on("dv"):
                W("dv", g8)
                eng.tensor_copy(colvb[0:1, 2:4],
                                srow[0:1, 66:68]).then_inc(S["dv"], 1)
            g9 = inc("dv")
            if dbg_tap:
                if on("dv"):
                    W("dv", g9)
                    eng.tensor_copy(c8dbg[0:1, 120:128],
                                    srow2[0:1, 16:24])
                    eng.tensor_copy(c8dbg[0:1, 28:32],
                                    colvb[0:1, 0:4]).then_inc(S["dv"], 1)
                g9 = inc("dv")
            if on("mm"):
                W("dv", g9)
                eng.matmul(spare[:, 16:18], onesb[:], colvb[0:1, 0:2],
                           start=True, stop=False)
                eng.matmul(spare[:, 16:18], onesb[:], colvb[0:1, 2:4],
                           start=False, stop=True).then_inc(S["mm"], 1)
            bcm = inc("mm")
            bce = act_copy(stats[:, 30:32], spare[:, 16:18], wait=[("mm", bcm)])
            if dbg_tap and on("dv"):
                W("ac", bce)
                eng.tensor_copy(c8dbg[:, 16:18], stats[:, 30:32])
                eng.tensor_copy(c8dbg[0:1, 18:26],
                                srow2[0:1, 16:24]).then_inc(S["dv"], 1)
            if dbg_tap:
                bce2 = inc("dv")
            else:
                bce2 = None
            if on("dv"):
                if bce2 is not None:
                    W("dv", bce2)
                else:
                    W("ac", bce)
                eng.tensor_scalar(src8, src8, stats[:, 30:31], stats[:, 31:32],
                                  op0=ALU.mult, op1=ALU.add).then_inc(S["dv"], 1)
            return inc("dv")

        if on("mm"):
            W("dv", p3["bankrd"])
            ins = None
            for dd in range(8):
                for cc in range(8):
                    ins = eng.matmul(bndp[:, dd:dd + 1],
                                     w3v[:, 0, cc, 128 * dd:128 * dd + 128],
                                     atn[:, cc, 2048:2049],
                                     start=(cc == 0), stop=(cc == 7))
            ins.then_inc(S["mm"], 1)
        c8mm = inc("mm")
        c8ev = act_copy(stats[:, 32:40], bndp[:, 0:8], wait=[("mm", c8mm)])
        if on("dv"):
            W("ac", c8ev)
            eng.tensor_tensor(stats[:, 32:40], stats[:, 32:40], vecs[:, 24:32],
                              op=ALU.add).then_inc(S["dv"], 1)
        c8b = inc("dv")
        c8ln = ln_col8(stats[:, 32:40], "dv", c8b)
        def W_gp_all():
            for i_ in range(NSLOT):
                nm = f"gp{i_}"
                if slot_uses[nm] > 0:
                    W(nm, 16 * slot_uses[nm])

        if on("gp"):
            W_gp_all()
            W("dv", p3["ln2_last"])
            W("ac", p3["acc2_last"])
        wxe = dma("gp", tm4[0:8, 1, 0:128],
                  dataclasses.replace(xsd[0:1, 0:1024],
                                      offset=xsd.offset + 2048 * 1024,
                                      ap=[[128, 8], [1, 128]]))
        if on("dv"):
            W(wxe)
            W("dv", c8ln)
            eng.tensor_copy(ptbF[0:8, 1024:1152],
                            tm4[0:8, 1, 0:128]).then_inc(S["dv"], 1)
        xh1a = inc("dv")
        if on("dv"):
            W("dv", xh1a)
            eng.scalar_tensor_tensor(ptbF[0:8, 1152:1280], ptbF[0:8, 1024:1152],
                                     -1.0, tm4[0:8, 1, 0:128], op0=ALU.mult,
                                     op1=ALU.add).then_inc(S["dv"], 1)
        xh1 = inc("dv")
        if on("mm"):
            W("dv", xh1)
            eng.matmul(spare[:, 24:32], ptbF[0:8, 1024:1152],
                       identb[0:8, 0:8], start=True, stop=False)
            eng.matmul(spare[:, 24:32], ptbF[0:8, 1152:1280],
                       identb[0:8, 0:8], start=False, stop=True).then_inc(S["mm"], 1)
        xtm = inc("mm")
        xte = act_copy(tm4[:, 0, 0:8], spare[:, 24:32], wait=[("mm", xtm)])
        if on("dv"):
            W("ac", xte)
            eng.tensor_tensor(tm4[:, 0, 0:8], stats[:, 32:40], tm4[:, 0, 0:8],
                              op=ALU.add).then_inc(S["dv"], 1)
        c8x2 = inc("dv")
        if on("dv"):
            W("dv", c8x2)
            eng.tensor_copy(ptbF[:, 1024:1032],
                            tm4[:, 0, 0:8]).then_inc(S["dv"], 1)
        xh2a = inc("dv")
        if on("dv"):
            W("dv", xh2a)
            eng.scalar_tensor_tensor(ptbF[:, 1032:1040], ptbF[:, 1024:1032],
                                     -1.0, tm4[:, 0, 0:8], op0=ALU.mult,
                                     op1=ALU.add).then_inc(S["dv"], 1)
        xh2 = inc("dv")
        if on("mm"):
            W("dv", xh2)
            eng.matmul(spare[0:8, 32:160], ptbF[:, 1024:1032],
                       identb[:], start=True, stop=False)
            eng.matmul(spare[0:8, 32:160], ptbF[:, 1032:1040],
                       identb[:], start=False, stop=True).then_inc(S["mm"], 1)
        xtm2 = inc("mm")
        xte2 = act_copy(tm4[0:8, 1, 0:128], spare[0:8, 32:160],
                        wait=[("mm", xtm2)])
        if on("gp"):
            W("ac", xte2)
        wx2e = dma("gp",
                   dataclasses.replace(x2d[0:1, 0:1024],
                                       offset=x2d.offset + 2048 * 1024,
                                       ap=[[128, 8], [1, 128]]),
                   tm4[0:8, 1, 0:128])
        if on("dv"):
            W("dv", c8x2)
            W("mm", xtm2)
            eng.tensor_copy(stats[:, 32:40], tm4[:, 0, 0:8]).then_inc(S["dv"], 1)
        c8cp = inc("dv")
        c8ln2 = ln_col8(stats[:, 32:40], "dv", c8cp)
        qcol_out = dataclasses.replace(qts[:, :, 2048:2049],
                                       ap=[[8 * 2052, 128], [2052, 8]])
        if on("dv"):
            W("dv", c8ln2)
            eng.tensor_copy(qcol_out, stats[:, 32:40]).then_inc(S["dv"], 1)
        c8st = inc("dv")
        mcol = dataclasses.replace(qts[:, :, 0:1], ap=[[8 * 2052, 128], [2052, 8]])
        if on("dv"):
            W("dv", c8st)
            W("ac", p3["btev"])
            eng.tensor_scalar(mcol, mcol, vecs[:, 48:49], None,
                              op0=ALU.mult).then_inc(S["dv"], 1)
        maskd = inc("dv")

        if on("dv"):
            W("dv", maskd)
            eng.tensor_scalar(stats[:, 0:8], vecs[:, 32:40], -1.0, 1.0,
                              op0=ALU.mult, op1=ALU.add)
            eng.tensor_scalar(stats[:, 8:16], vecs[:, 40:48], -1.0, 1.0,
                              op0=ALU.mult, op1=ALU.add).then_inc(S["dv"], 1)
        onem_d = inc("dv")

        f3 = {"xr": 0, "kact": 0, "hilo": 0, "ytt": 0, "ydma": [],
              "x2rdw": None, "tp_last": 0, "bankrd": 0,
              "opev": {0: 0, 1: 0, 2: 0, 3: 0}, "rfree": 0}

        for t3 in range(4):
            c1 = 1 + 512 * t3
            if on("gp"):
                W("mm", c8mm)
                W("dv", f3["ytt"])
                W_gp_all()
            wx2r = dma("gp", x2rd.rearrange("p (a c) -> p a c", c=1024),
                       x2d[c1:c1 + 512, :].rearrange("(a p) c -> p a c", p=128))
            f3["x2rdw"] = wx2r
            if on("dv"):
                W("dv", onem_d)
                if t3 == 0:
                    W("dv", p3["hilo"])
                    W("mm", p3["tp_last"])
                else:
                    W("mm", f3["tp_last"])
                ins = None
                for cc in range(8):
                    eng.tensor_scalar(xrv[:, cc, :], qts[:, cc, c1:c1 + 512],
                                      vecs[:, 32 + cc:33 + cc], None,
                                      op0=ALU.mult)
                    ins = eng.tensor_scalar(xkv[:, cc, :],
                                            qts[:, cc, c1:c1 + 512],
                                            vecs[:, 40 + cc:41 + cc], None,
                                            op0=ALU.mult)
                ins.then_inc(S["dv"], 1)
            mix_a = inc("dv")
            if on("dv"):
                W("dv", mix_a)
                ins = None
                for cc in range(8):
                    eng.scalar_tensor_tensor(xrv[:, cc, :],
                                             qts[:, cc, c1 - 1:c1 + 511],
                                             stats[:, cc:cc + 1],
                                             xrv[:, cc, :], op0=ALU.mult,
                                             op1=ALU.add)
                    ins = eng.scalar_tensor_tensor(xkv[:, cc, :],
                                                   qts[:, cc, c1 - 1:c1 + 511],
                                                   stats[:, 8 + cc:9 + cc],
                                                   xkv[:, cc, :], op0=ALU.mult,
                                                   op1=ALU.add)
                ins.then_inc(S["dv"], 1)
            f3["xr"] = inc("dv")
            for dd in range(8):
                bank = banksl[dd % 4]
                if on("mm"):
                    if dd == 0:
                        W("dv", f3["xr"])
                        if t3 == 0:
                            W("ac", p3["btev"])
                            for t_ in w3l[1:2]:
                                W(t_)
                        else:
                            W("dv", f3["vev7"])
                    W("ac", f3["opev"][dd % 4])
                    ins = None
                    for cc in range(8):
                        ins = eng.matmul(bank,
                                         w3v[:, 1, cc, 128 * dd:128 * dd + 128],
                                         xrv[:, cc, :], start=(cc == 0),
                                         stop=(cc == 7))
                    ins.then_inc(S["mm"], 1)
                mmid = inc("mm")
                if on("ac"):
                    W("mm", mmid)
                    if dd == 0:
                        W("dv", f3["rfree"])
                        for t_ in f3["ydma"][-4:]:
                            W(t_)
                    eng.activation(rfv[:, dd, :], bank,
                                   AF.Sigmoid).then_inc(S["ac"], 1)
                f3["opev"][dd % 4] = inc("ac")
            kacts = {}
            for dd in range(8):
                bank = banksl[dd % 4]
                if on("mm"):
                    if dd == 0:
                        for t_ in w3l[2:3]:
                            W(t_)
                    W("ac", f3["opev"][dd % 4])
                    if dd >= 4:
                        W("dv", kacts[dd - 4])
                    ins = None
                    for cc in range(8):
                        ins = eng.matmul(bank,
                                         w3v[:, 2, cc, 128 * dd:128 * dd + 128],
                                         xkv[:, cc, :], start=(cc == 0),
                                         stop=(cc == 7))
                    ins.then_inc(S["mm"], 1)
                mmid = inc("mm")
                if on("dv"):
                    W("mm", mmid)
                    eng.tensor_scalar(xrv[:, dd, :], bank, 0.0, None,
                                      op0=ALU.max).then_inc(S["dv"], 1)
                krl = inc("dv")
                if on("dv"):
                    W("dv", krl)
                    eng.tensor_tensor(xrv[:, dd, :], xrv[:, dd, :],
                                      xrv[:, dd, :],
                                      op=ALU.mult).then_inc(S["dv"], 1)
                kacts[dd] = inc("dv")
                f3["opev"][dd % 4] = 0
            f3["kact"] = kacts[7]
            vevs = {}
            for dd in range(8):
                bank = banksl[dd % 4]
                if on("mm"):
                    if dd == 0:
                        W("dv", f3["kact"])
                        for t_ in w3l[3:4]:
                            W(t_)
                    if dd >= 4:
                        W("dv", vevs[dd - 4])
                    ins = None
                    for cc in range(8):
                        ins = eng.matmul(bank,
                                         w3v[:, 3, cc, 128 * dd:128 * dd + 128],
                                         xrv[:, cc, :], start=(cc == 0),
                                         stop=(cc == 7))
                    ins.then_inc(S["mm"], 1)
                mmid = inc("mm")
                if on("dv"):
                    W("mm", mmid)
                    if dd == 0 and t3 == 0:
                        W("dv", p3["ln2_last"])
                        W("ac", p3["acc2_last"])
                        W(wx2e)
                        for t_ in [p3["x2slot"]]:
                            W(t_)
                    eng.tensor_tensor(tm8[:, dd, :], bank, rfv[:, dd, :],
                                      op=ALU.mult).then_inc(S["dv"], 1)
                vevs[dd] = inc("dv")
            f3["vev7"] = vevs[7]
            vmm_last = cnt["mm"]
            if on("dv"):
                W("dv", vevs[7])
                W("mm", vmm_last)
                eng.tensor_copy(xrv[:, :, :], tm8[:, :, :]).then_inc(S["dv"], 1)
            fhi_a = inc("dv")
            if on("dv"):
                W("dv", fhi_a)
                eng.scalar_tensor_tensor(xkv[:, :, :], xrv[:, :, :], -1.0,
                                         tm8[:, :, :], op0=ALU.mult,
                                         op1=ALU.add).then_inc(S["dv"], 1)
            f3["hilo"] = inc("dv")
            for s3 in range(4):
                if on("mm"):
                    W("dv", f3["hilo"])
                    W("dv", f3["bankrd"])
                    ins = None
                    for dd in range(8):
                        bank = bndp if dd < 4 else spare
                        col = 128 * (dd % 4)
                        eng.matmul(bank[:, col:col + 128],
                                   xrv[:, dd, 128 * s3:128 * s3 + 128],
                                   identb[:], start=True, stop=False)
                        ins = eng.matmul(bank[:, col:col + 128],
                                         xkv[:, dd, 128 * s3:128 * s3 + 128],
                                         identb[:], start=False, stop=True)
                    ins.then_inc(S["mm"], 1)
                tp = inc("mm")
                f3["tp_last"] = tp
                lnp = ln_psum(16, tp)
                if on("dv"):
                    W("dv", lnp)
                    W(wx2r)
                    eng.tensor_tensor(rff[:, 1024 * s3:1024 * s3 + 512],
                                      bndp[:, 0:512],
                                      x2rd[:, 1024 * s3:1024 * s3 + 512],
                                      op=ALU.add)
                    eng.tensor_tensor(rff[:, 1024 * s3 + 512:1024 * s3 + 1024],
                                      spare[:, 0:512],
                                      x2rd[:, 1024 * s3 + 512:1024 * s3 + 1024],
                                      op=ALU.add).then_inc(S["dv"], 1)
                ytt = inc("dv")
                f3["ytt"] = ytt
                f3["bankrd"] = ytt
                if on("gp"):
                    W("dv", ytt)
                wy = dma("gp",
                         y_out[512 * t3 + 128 * s3:512 * t3 + 128 * s3 + 128, :],
                         rff[:, 1024 * s3:1024 * s3 + 1024])
                f3["ydma"].append(wy)
            f3["rfree"] = f3["ytt"]

        keys = (["x2", "qt"] if debug else [])
        return finish(me, eng, cnt, S, keys)

    PH_LT3 = [phases < 4]

    def finish(me, eng, cnt, S, debug_keys):
        last_st = 0
        srcs = {
            "xlnt": xlntd, "kt": ktd, "vsw": vsw, "p8f": p8f, "p8r": p8r,
            "x2": x2d[0:2049, :],
            "qt": qts[:], "sc": srow[:], "vtab": vtab[:], "vtab1": vtab1[:],
            "c8dbg": c8dbg[:],
            "atn": atn[:],
        }
        if me == "sy":
            for k in sem_names:
                if k == "st":
                    continue
                if cnt.get(k, 0) > 0:
                    eng.wait_ge(S[k], cnt[k])
        for key in debug_keys:
            if me == "sy":
                if last_st:
                    eng.wait_ge(S["st"], last_st)
                eng.dma_start(out=dbg[key], in_=srcs[key]).then_inc(S["st"], 16)
            cnt["st"] += 16
            last_st = cnt["st"]
        if PH_LT3[0]:
            if me == "sy":
                if last_st:
                    eng.wait_ge(S["st"], last_st)
                eng.dma_start(out=y_out[0:128, :],
                              in_=pf32[:, 0:1024]).then_inc(S["st"], 16)
            cnt["st"] += 16
            last_st = cnt["st"]
        if me == "sy" and last_st:
            eng.wait_ge(S["st"], last_st)
        return cnt

    @block.sync
    def _(sync):
        walk("sy", sync)

    @block.gpsimd
    def _(gpsimd):
        walk("gp", gpsimd)

    @block.tensor
    def _(tensor):
        walk("mm", tensor)

    @block.scalar
    def _(scalar):
        walk("ac", scalar)

    @block.vector
    def _(vector):
        walk("dv", vector)

    ctx.close()
    return nc



def _prep_inputs(inputs):
    x = np.ascontiguousarray(inputs["L0_attn_x"])
    tw = lambda w: np.ascontiguousarray(np.asarray(w).T).astype(np.float32, copy=False)
    wq, wk, wv, wo = (tw(inputs["L0_attn_wq"]), tw(inputs["L0_attn_wk"]),
                      tw(inputs["L0_attn_wv"]), tw(inputs["L0_attn_wo"]))
    wr, wkf, wvf = (tw(inputs["L0_ffn_W_r"]), tw(inputs["L0_ffn_W_k"]),
                    tw(inputs["L0_ffn_W_v"]))
    relb = np.ascontiguousarray(np.asarray(inputs["L0_attn_rel_bias"],
                                           np.float32))
    mf = np.ascontiguousarray(_m_fwd()[:, ::-1])
    ident = np.eye(128).astype(ml_dtypes.bfloat16)

    def chunkvec(v):
        return np.ascontiguousarray(np.asarray(v, np.float32).reshape(8, 128).T)

    vecs = np.zeros((128, 56), np.float32)
    vecs[:, 0:8] = chunkvec(inputs["L0_attn_bq"])
    vecs[:, 8:16] = chunkvec(inputs["L0_attn_bk"])
    vecs[:, 16:24] = chunkvec(inputs["L0_attn_bv"])
    vecs[:, 24:32] = chunkvec(inputs["L0_attn_bo"])
    vecs[:, 32:40] = chunkvec(inputs["L0_ffn_mu_r"])
    vecs[:, 40:48] = chunkvec(inputs["L0_ffn_mu_k"])
    bvrow = np.ascontiguousarray(
        np.asarray(inputs["L0_attn_bv"], np.float32).reshape(1, D))

    in_maps = []
    for c in range(NCORES):
        b, j = divmod(c, 4)
        im = {
            "xb": np.ascontiguousarray(x[b]).astype(np.float32, copy=False),
            "wqT": wq, "wkT": wk, "wvT": wv, "woT": wo,
            "wrT": wr, "wkfT": wkf, "wvfT": wvf,
            "vecs": np.ascontiguousarray(vecs.copy()),
            "bvrow": bvrow,
            "relb": relb,
            "mf": mf,
            "mr": _m_rev(2048 * j),
            "clsm": _clsmask(j),
            "identb": ident,
        }
        m = im["vecs"]
        m[:, 48] = 0.0 if j == 0 else 1.0
        in_maps.append(im)
    return in_maps


def kernel(**inputs):
    in_maps = _prep_inputs(inputs)
    nc = build_program(debug=DEBUG, phases=PHASES)
    res = run_bass_kernel_spmd(nc, in_maps, core_ids=list(range(NCORES)))
    out = np.zeros((2, L, D), np.float32)
    for c in range(NCORES):
        b, j = divmod(c, 4)
        out[b, 2048 * j:2048 * j + 2048, :] = res.results[c]["y"]
    if DEBUG:
        kernel._last = res
    return out


# version 07 (final)
# speedup vs baseline: 1.1135x; 1.1135x over previous
import dataclasses
import numpy as np
import ml_dtypes

import concourse.bass as bass
import concourse.mybir as mybir
from concourse.bass_utils import run_bass_kernel_spmd

F32 = mybir.dt.float32
BF16 = mybir.dt.bfloat16
AF = mybir.ActivationFunctionType
ALU = mybir.AluOpType
AX = mybir.AxisListType

NCORES = 8
L, D = 8192, 1024
NHP = 8
NCH = 64
NT = 8
NQT = 4
NEAR = (0, 1, 2, 3, 63)

DEBUG = False
PHASES = 4
SUBPH = 99
PH2N = 8
PH2M = 9
PH2X = 7


def _bucket(d):
    n = np.asarray(d, dtype=np.int64)
    is_small = n < 16
    nf = np.maximum(n, 1).astype(np.float64)
    val = 16 + np.trunc(np.log(nf / 16.0) / np.log(8.0) * 16.0).astype(np.int64)
    val = np.minimum(val, 31)
    ret = np.where(is_small, n, val)
    return np.maximum(ret, 0)


def _m_fwd():
    i = np.arange(16384)
    d = i - 8192
    bk = np.where(d <= 0, 0, _bucket(d))
    M = np.zeros((32, 16384), np.float32)
    M[bk, i] = 8.0
    M[0, :] -= 8.0
    return M


def _m_rev(t0):
    y = np.arange(8192)
    dy = (t0 - 1) - ((y + t0) % 8192)
    bk = np.where(dy <= 0, 0, _bucket(dy))
    M = np.zeros((32, 8192), np.float32)
    M[bk, y] = 8.0
    M[0, :] -= 8.0
    if t0 == 0:
        M[:] = 0.0
    return M


def _clsmask(j):
    m = np.zeros((4, 64), np.float32)
    for qt in range(4):
        s = 16 * j + 4 * qt
        for c2 in range(64):
            if c2 not in NEAR and c2 + s >= 64:
                m[qt, c2] = 1.0
    return np.ascontiguousarray(m.reshape(1, 256))


def build_program(debug=False, phases=3):
    nc = bass.Bass("TRN2", target_bir_lowering=False, debug=False)

    xb = nc.dram_tensor("xb", [L, D], F32, kind="ExternalInput").ap()
    wqT = nc.dram_tensor("wqT", [D, D], F32, kind="ExternalInput").ap()
    wkT = nc.dram_tensor("wkT", [D, D], F32, kind="ExternalInput").ap()
    wvT = nc.dram_tensor("wvT", [D, D], F32, kind="ExternalInput").ap()
    woT = nc.dram_tensor("woT", [D, D], F32, kind="ExternalInput").ap()
    wrT = nc.dram_tensor("wrT", [D, D], F32, kind="ExternalInput").ap()
    wkfT = nc.dram_tensor("wkfT", [D, D], F32, kind="ExternalInput").ap()
    wvfT = nc.dram_tensor("wvfT", [D, D], F32, kind="ExternalInput").ap()
    vecs_in = nc.dram_tensor("vecs", [128, 56], F32, kind="ExternalInput").ap()
    bvrow_in = nc.dram_tensor("bvrow", [1, D], F32, kind="ExternalInput").ap()
    relb_in = nc.dram_tensor("relb", [32, 16], F32, kind="ExternalInput").ap()
    mf_in = nc.dram_tensor("mf", [32, 16384], F32, kind="ExternalInput").ap()
    mr_in = nc.dram_tensor("mr", [32, 8192], F32, kind="ExternalInput").ap()
    clsm_in = nc.dram_tensor("clsm", [1, 256], F32, kind="ExternalInput").ap()
    ident_in = nc.dram_tensor("identb", [128, 128], BF16, kind="ExternalInput").ap()
    y_out = nc.dram_tensor("y", [2048, D], F32, kind="ExternalOutput").ap()

    ktd = nc.dram_tensor("ktd", [NHP, 128, 2 * L], BF16, kind="Internal").ap()
    vsw = nc.dram_tensor("vsw", [2 * NCH, 128, 1040], BF16, kind="Internal").ap()
    xlntd = nc.dram_tensor("xlntd", [8, 128, L], BF16, kind="Internal").ap()
    p8f = nc.dram_tensor("p8f", [16, 16384], F32, kind="Internal").ap()
    p8r = nc.dram_tensor("p8r", [16, 8192], F32, kind="Internal").ap()
    x2d = nc.dram_tensor("x2d", [2052, D], F32, kind="Internal").ap()
    xsd = nc.dram_tensor("xsd", [2052, D], F32, kind="Internal").ap()
    w3d = nc.dram_tensor("w3d", [4, 128, 8, D], BF16, kind="Internal").ap()

    dbg = {}
    if debug:
        def dout(name, shape, dt):
            dbg[name] = nc.dram_tensor("dbg_" + name, shape, dt,
                                       kind="ExternalOutput").ap()
        dout("xlnt", [8, 128, L], BF16)
        dout("qt", [128, 8, 2052], BF16)
        dout("kt", [NHP, 128, 2 * L], BF16)
        dout("vsw", [2 * NCH, 128, 1040], BF16)
        dout("p8f", [16, 16384], F32)
        dout("p8r", [16, 8192], F32)
        dout("sc", [1, 256], F32)
        dout("vtab", [128, 2048], F32)
        dout("vtab1", [128, 2048], F32)
        dout("atn", [128, 8, 2052], BF16)
        dout("x2", [2049, D], F32)
        dout("c8dbg", [128, 128], F32)

    from contextlib import ExitStack
    ctx = ExitStack()
    sbT = lambda *a: ctx.enter_context(nc.sbuf_tensor(*a))
    psT = lambda *a: ctx.enter_context(nc.psum_tensor(*a))

    qts = sbT("qts", [128, 8, 2052], BF16)
    atn = sbT("atn", [128, 8, 2052], BF16)
    pool = sbT("pool", [128, 57344], BF16)
    ptb = sbT("ptb", [128, 2, 2, 512], BF16)
    vecs = sbT("vecs_s", [128, 56], F32)
    identb = sbT("ident_s", [128, 128], BF16)
    relb = sbT("relb_s", [32, 16], F32)
    relbb = sbT("relbb_s", [32, 16], BF16)
    clsm = sbT("clsm_s", [1, 256], F32)
    bvrb = sbT("bvrb", [1, D], BF16)
    sums = sbT("sums", [128, 8, 2], F32)
    stats = sbT("stats", [128, 48], F32)
    vtab = sbT("vtab", [128, 2048], F32)
    vtab1 = sbT("vtab1", [128, 2048], F32)
    c8dbg = sbT("c8dbg", [128, 128], F32)
    srow = sbT("srow", [1, 256], F32)
    srow2 = sbT("srow2", [1, 256], F32)
    onesf = sbT("onesf", [1, 128], F32)
    onesb = sbT("onesb", [1, 128], BF16)
    onescol = sbT("onescol", [128, 1], BF16)
    onesw = sbT("onesw", [128, 64], BF16)
    mkac = sbT("mkac", [1, 80], F32)
    colv = sbT("colv", [128, 4], F32)
    colvb = sbT("colvb", [128, 4], BF16)
    bb_st = sbT("bb_st", [128, 128], F32)
    ptbd = sbT("ptbd", [128, 128], BF16)

    pf32 = pool[:].bitcast(F32)
    pool_flat = pool[:]
    mrow = pf32[0:32, 3072:3584]
    mrowb = pool_flat[0:32, 8192:8704]
    prow = pf32[0:16, 3584:4096]
    rbB = pf32[:, 22656:23168]
    bbld = pf32[0:2, 23168:24192]
    bbldb = pool_flat[0:2, 48384:49408]

    P1_XF32 = 0
    P1_XLN = 16384
    P1_XLNT = 24576
    P1_WVT = 32768
    P1_VST = 40960
    P1_KST = 42048
    P1_QSL = 0

    P2_KT = 0
    P2_V = 16384
    P2_B8F = 16512

    P3_W = 0
    P3_RF = 16384
    P3_XR = 40960
    P3_XK = 45056
    P3_KA = 45056
    P3_TMPF = 24576

    s_a = psT("s_a", [128, 2, 512], F32)
    s_b = psT("s_b", [128, 2, 512], F32)
    pv0 = psT("pv0", [65, 512], F32)
    pv1 = psT("pv1", [65, 512], F32)
    bndp = psT("bndp", [128, 512], F32)
    spare = psT("spare", [128, 512], F32)

    NSLOT = 6
    sem_names = (["mm", "ac", "dv", "st", "ini", "sx", "pl"]
                 + [f"sy{i}" for i in range(NSLOT)]
                 + [f"gp{i}" for i in range(NSLOT)])
    sems = {k: ctx.enter_context(nc.semaphore(k)) for k in sem_names}
    block = ctx.enter_context(nc.Block())

    atn_b = atn[:].rearrange("p a b -> p (a b)")

    def walk(me, eng):
        S = sems
        cnt = {k: 0 for k in sem_names}

        def inc(key, amt=1):
            cnt[key] += amt
            return cnt[key]

        def on(name):
            return me == name

        def W(key, val=None):
            if val is None:
                key, val = key
            if val > 0:
                eng.wait_ge(S[key], val)

        dma_rr = {"sy": 0, "gp": 0}
        slot_uses = {k: 0 for k in sem_names}

        def dma(queue, out, in_):
            name = f"{queue}{dma_rr[queue] % NSLOT}"
            dma_rr[queue] += 1
            if on(queue):
                if slot_uses[name] > 0:
                    W(name, 16 * slot_uses[name])
                q = eng
                q.dma_start(out=out, in_=in_).then_inc(S[name], 16)
            slot_uses[name] += 1
            cnt[name] = 16 * slot_uses[name]
            return (name, 16 * slot_uses[name])

        rt = {}

        def get_rt():
            if "t0" not in rt:
                pid = eng.partition_id()
                j = pid % 4
                rt["j"] = j
                rt["t0"] = j * 2048
                rt["qoff"] = j * 2048 - (j >= 1)
            return rt

        def guard(key):
            pass

        def act_copy(dst, src, bias=None, wait=()):
            if on("ac"):
                for k, v in wait:
                    W(k, v)
                if bias is None:
                    eng.activation(dst, src, AF.Identity).then_inc(S["ac"], 1)
                else:
                    eng.activation(dst, src, AF.Identity,
                                   bias=bias).then_inc(S["ac"], 1)
            return inc("ac")

        def part_row(src_col_f32, spare_off, wait_dv):
            ci = spare_off % 4
            if on("dv"):
                W("dv", wait_dv)
                eng.tensor_copy(colvb[:, ci:ci + 1], src_col_f32).then_inc(S["dv"], 1)
            dvt = inc("dv")
            if on("mm"):
                W("dv", dvt)
                eng.matmul(spare[0:1, 128 * ci:128 * ci + 128], colvb[:, ci:ci + 1],
                           identb[:], start=True, stop=True).then_inc(S["mm"], 1)
            mmt = inc("mm")
            return mmt, spare[0:1, 128 * ci:128 * ci + 128]

        def dma_ini(out, in_):
            if on("sy"):
                eng.dma_start(out=out, in_=in_).then_inc(S["ini"], 16)
            return inc("ini", 16)

        dma_ini(vecs[:], vecs_in)
        dma_ini(identb[:], ident_in)
        dma_ini(relb[:], relb_in)
        dma_ini(clsm[:], clsm_in)
        ld1 = dma_ini(pf32[0:1, 2048:3072], bvrow_in)
        if on("sy"):
            r = get_rt()
            s1 = dataclasses.replace(xb[0:2048, :], offset=xb.offset + r["t0"] * 1024)
            eng.dma_start(out=xsd[1:2049, :], in_=s1).then_inc(S["sx"], 16)
            s0 = dataclasses.replace(xb[0:1, :], offset=xb.offset + r["qoff"] * 1024)
            eng.dma_start(out=xsd[0:1, :], in_=s0).then_inc(S["sx"], 16)
        xsd_w = inc("sx", 32)

        if on("dv"):
            W("ini", ld1)
            eng.tensor_copy(relbb[:], relb[:])
            eng.tensor_copy(bvrb[:], pf32[0:1, 2048:3072])
            eng.memset(onesf[:], 1.0)
            eng.memset(onesb[:], 1.0)
            eng.memset(onescol[:], 1.0)
            eng.memset(onesw[:], 1.0)
            eng.memset(srow[:], 0.0)
            eng.memset(srow2[:], 0.0)
            eng.memset(c8dbg[:], 0.0)
            eng.memset(colv[:], 0.0)
            eng.memset(qts[:], 0.0)
            eng.memset(atn[:], 0.0)
            eng.memset(mkac[:], 0.0).then_inc(S["dv"], 1)
        dv_init = inc("dv")

        ppst = {"cst": dv_init, "ev": 0, "pdma": ("ini", 0), "mmw": 0}

        def profile_pass(m_src, n_chunks, dst):
            for ch in range(n_chunks):
                if on("sy"):
                    W("dv", ppst["cst"])
                w = dma("sy", mrow, m_src[:, 512 * ch:512 * ch + 512])
                if on("dv"):
                    W(w)
                    W("mm", ppst["mmw"])
                    eng.tensor_copy(mrowb, mrow).then_inc(S["dv"], 1)
                ppst["cst"] = inc("dv")
                if on("mm"):
                    W("dv", ppst["cst"])
                    W("ac", ppst["ev"])
                    eng.matmul(spare[0:16, :], relbb[:], mrowb,
                               start=True, stop=True).then_inc(S["mm"], 1)
                mmc = inc("mm")
                ppst["mmw"] = mmc
                ppst["ev"] = act_copy(prow, spare[0:16, :],
                                      wait=[("mm", mmc), ppst["pdma"]])
                if on("gp"):
                    W("ac", ppst["ev"])
                ppst["pdma"] = dma("gp", dst[:, 512 * ch:512 * ch + 512], prow)
            return cnt["ac"]

        if SUBPH < 2:
            return finish(me, eng, cnt, S, [])
        if SUBPH == 33:
            profile_pass(mf_in, 16, p8f)
            return finish(me, eng, cnt, S, [])
        if SUBPH == 34:
            profile_pass(mf_in, 8, p8f)
            profile_pass(mf_in, 8, p8f)
            return finish(me, eng, cnt, S, [])
        if SUBPH == 35:
            profile_pass(mf_in, 24, p8f)
            return finish(me, eng, cnt, S, [])
        if SUBPH == 36:
            profile_pass(mf_in, 32, p8f)
            profile_pass(mf_in, 2, p8f)
            return finish(me, eng, cnt, S, [])
        pp1 = profile_pass(mf_in, 32, p8f)
        if SUBPH < 3:
            return finish(me, eng, cnt, S, [])
        if SUBPH == 31:
            pp2 = profile_pass(mf_in, 16, p8f)
            return finish(me, eng, cnt, S, [])
        if SUBPH == 32:
            pp2 = profile_pass(mr_in, 16, p8f)
            return finish(me, eng, cnt, S, [])
        pp2 = profile_pass(mr_in, 16, p8r)
        if SUBPH < 4:
            return finish(me, eng, cnt, S, [])

        cast_w_dv = ppst["cst"]
        for wi, wsrc in enumerate([wqT, wkT, wvT]):
            for cc in range(8):
                if on("sy"):
                    W("dv", cast_w_dv)
                w = dma("sy", pf32[:, 0:1024],
                        wsrc[128 * cc:128 * cc + 128, :])
                if on("dv"):
                    W(w)
                    if wi < 2:
                        dst = atn_b[:, 8192 * wi + 1024 * cc:8192 * wi + 1024 * cc + 1024]
                    else:
                        dst = pool_flat[:, P1_WVT + 1024 * cc:P1_WVT + 1024 * cc + 1024]
                    eng.tensor_copy(dst, pf32[:, 0:1024]).then_inc(S["dv"], 1)
                cast_w_dv = inc("dv")

        if SUBPH < 5:
            return finish(me, eng, cnt, S, [])
        w3prev = {"cst": cast_w_dv, "dma": ("ini", 0)}
        for wi, wsrc in enumerate([woT, wrT, wkfT, wvfT]):
            for cc in range(8):
                if on("sy"):
                    W("dv", w3prev["cst"])
                w = dma("sy", pf32[:, 1024:2048], wsrc[128 * cc:128 * cc + 128, :])
                if on("dv"):
                    W(w)
                    W(w3prev["dma"])
                    eng.tensor_copy(pool_flat[:, P1_KST:P1_KST + 1024],
                                    pf32[:, 1024:2048]).then_inc(S["dv"], 1)
                cst = inc("dv")
                w3prev["cst"] = cst
                if on("gp"):
                    if wi == 0 and cc == 0:
                        W("ac", pp2)
                    W("dv", cst)
                w3prev["dma"] = dma("gp", w3d[wi, :, cc, :],
                                    pool_flat[:, P1_KST:P1_KST + 1024])

        wcast_dv = cnt["dv"]
        w3_dma_last = w3prev["dma"]

        if phases < 1:
            return finish(me, eng, cnt, S, [])

        xf32v = pf32[:, 0:8192].rearrange("p (a c) -> p a c", c=1024)
        xlnv = pool_flat[:, P1_XLN:P1_XLN + 8192].rearrange("p (a c) -> p a c", c=1024)
        xlntv = pool_flat[:, P1_XLNT:P1_XLNT + 8192].rearrange("p (a c) -> p a c", c=1024)
        wvtv = pool_flat[:, P1_WVT:P1_WVT + 8192].rearrange("p (a c) -> p a c", c=1024)
        vstv = pool_flat[:, P1_VST:P1_VST + 1040].rearrange("p (h e) -> p h e", e=65)
        vst_flat = pool_flat[:, P1_VST:P1_VST + 1040]
        kstv = pool_flat[:, P1_KST:P1_KST + 1024]
        wqTb = atn_b[:, 0:8192].rearrange("p (cc d) -> p cc d", d=1024)
        wkTb = atn_b[:, 8192:16384].rearrange("p (cc d) -> p cc d", d=1024)

        def ln_tokenmajor(src_fn, n_sub, dst_fn, first_wait,
                          dummy=None, norm_wait=()):
            for a in range(n_sub):
                if on("ac"):
                    if a == 0:
                        for k, v in first_wait:
                            W(k, v)
                    eng.activation(dummy(a, 0), src_fn(a), AF.Identity,
                                   accum_out=sums[:, a, 0:1])
                    ins = eng.activation(dummy(a, 1), src_fn(a), AF.Square,
                                         accum_out=sums[:, a, 1:2])
                    if a == n_sub - 1:
                        ins.then_inc(S["ac"], 1)
            acc = inc("ac")
            if on("dv"):
                W("ac", acc)
                eng.tensor_scalar(stats[:, 0:n_sub], sums[:, 0:n_sub, 0],
                                  -1.0 / 1024, None, op0=ALU.mult)
                eng.tensor_scalar(stats[:, 8:8 + n_sub], sums[:, 0:n_sub, 1],
                                  1.0 / 1024, None, op0=ALU.mult).then_inc(S["dv"], 1)
            d1 = inc("dv")
            if on("dv"):
                W("dv", d1)
                eng.tensor_tensor(stats[:, 16:16 + n_sub], stats[:, 0:n_sub],
                                  stats[:, 0:n_sub], op=ALU.mult).then_inc(S["dv"], 1)
            d2 = inc("dv")
            if on("dv"):
                W("dv", d2)
                eng.scalar_tensor_tensor(stats[:, 8:8 + n_sub], stats[:, 8:8 + n_sub],
                                         1e-5, stats[:, 16:16 + n_sub],
                                         op0=ALU.add, op1=ALU.subtract).then_inc(S["dv"], 1)
            d3 = inc("dv")
            if on("ac"):
                W("dv", d3)
                eng.activation(stats[:, 24:24 + n_sub], stats[:, 8:8 + n_sub],
                               AF.Ln).then_inc(S["ac"], 1)
            a1 = inc("ac")
            if on("ac"):
                W("ac", a1)
                eng.activation(stats[:, 16:16 + n_sub], stats[:, 24:24 + n_sub],
                               AF.Exp, scale=-0.5).then_inc(S["ac"], 1)
            a2 = inc("ac")
            for a in range(n_sub):
                if on("dv"):
                    if a == 0:
                        W("ac", a2)
                        for k_, v_ in norm_wait:
                            W(k_, v_)
                    ins = eng.tensor_scalar(dst_fn(a), src_fn(a),
                                            stats[:, a:a + 1],
                                            stats[:, 16 + a:17 + a],
                                            op0=ALU.add, op1=ALU.mult)
                    if a == n_sub - 1:
                        ins.then_inc(S["dv"], 1)
            return inc("dv")

        def transpose8(src_fn, dst_fn, n_blocks, entry_ac):
            evs = {}
            ng = (n_blocks + 3) // 4
            for g in range(ng):
                sz = min(4, n_blocks - 4 * g)
                bank = bndp if g % 2 == 0 else spare
                if on("mm"):
                    if g < 2:
                        W("ac", entry_ac)
                    else:
                        W("ac", evs[g - 2])
                for i in range(sz):
                    if on("mm"):
                        ins = eng.matmul(bank[:, 128 * i:128 * i + 128],
                                         src_fn(4 * g + i), identb[:],
                                         start=True, stop=True)
                        if i == sz - 1:
                            ins.then_inc(S["mm"], 1)
                mmc = inc("mm")
                evs[g] = act_copy(
                    dst_fn(g, sz),
                    bank[:, 0:128 * sz].rearrange("p (cc t) -> p cc t", t=128),
                    wait=[("mm", mmc)])
            return evs[ng - 1]

        prev = {"kev": 0, "kdma": (), "kgp": 0, "vev": 0, "vdma": (),
                "xlnt_dma": ("ini", 0), "ln_done": 0, "mm_tile": 0}

        xq32 = qts[:].rearrange("p a c -> p (a c)").bitcast(F32)[
            :, 0:8192].rearrange("p (a c) -> p a c", c=1024)
        lnP = {0: 0, 1: 0}
        xwP = {}

        def emit_load(t):
            b = xq32 if t % 2 == 0 else xf32v
            if on("sy"):
                if t >= 2:
                    W("dv", lnP[t % 2])
                elif t == 0:
                    W("dv", dv_init)
                else:
                    W("dv", wcast_dv)
            return dma("sy", b[:, :, :],
                       xb[1024 * t:1024 * t + 1024, :].rearrange(
                           "(a p) c -> p a c", p=128))

        for T in range(NT):
            if T == 0:
                xwP[0] = emit_load(0)
                xwP[1] = emit_load(1)
            elif T + 1 < NT:
                xwP[(T + 1) % 2] = emit_load(T + 1)
            bufT = xq32 if T % 2 == 0 else xf32v
            first_w = [xwP[T % 2]]
            if T == 0:
                first_w.append(("dv", wcast_dv))
            ln_done = ln_tokenmajor(lambda a: bufT[:, a, :], 8,
                                    lambda a: xlnv[:, a, :], first_w,
                                    dummy=lambda a, k: ptb[:, k, :, :],
                                    norm_wait=([("mm", prev["mm_tile"])]
                                               if T > 0 else ()))
            prev["ln_done"] = ln_done
            lnP[T % 2] = ln_done

            def t_src(blk):
                a, ccb = divmod(blk, 8)
                return xlnv[:, a, 128 * ccb:128 * ccb + 128]

            def t_dst(g, sz):
                blk0 = 4 * g
                a, ccb0 = divmod(blk0, 8)
                return xlntv[:, ccb0:ccb0 + sz, 128 * a:128 * a + 128]

            if on("mm"):
                W("dv", ln_done)
            if on("ac") and T > 0:
                W(prev["xlnt_dma"])
            tr_done = transpose8(t_src, t_dst, 64, entry_ac=cnt["ac"])

            if on("gp"):
                W("ac", tr_done)
            prev["xlnt_dma"] = dma(
                "gp",
                xlntd[:, :, 1024 * T:1024 * T + 1024].rearrange("cc p c -> p cc c"),
                xlntv[:])

            for hp in range(NHP):
                if on("mm"):
                    if hp == 0:
                        W("ac", tr_done)
                    else:
                        W("ac", prev["kev"])
                for h2 in range(2):
                    for cc in range(8):
                        if on("mm"):
                            ins = eng.matmul(
                                s_a[:, h2, :],
                                wkTb[:, cc, 128 * hp:128 * hp + 128],
                                xlntv[:, cc, 512 * h2:512 * h2 + 512],
                                start=(cc == 0), stop=(cc == 7))
                            if h2 == 1 and cc == 7:
                                ins.then_inc(S["mm"], 1)
                mmk = inc("mm")
                if on("ac"):
                    W("mm", mmk)
                    if hp > 0 or T > 0:
                        for t_ in prev["kdma"]:
                            W(t_)
                        W("pl", prev["kgp"])
                    else:
                        W(w3_dma_last)
                    eng.activation(kstv[0:128, 0:512], s_a[:, 0, :], AF.Identity,
                                   bias=vecs[:, 8 + hp:9 + hp])
                    eng.activation(kstv[0:128, 512:1024], s_a[:, 1, :], AF.Identity,
                                   bias=vecs[:, 8 + hp:9 + hp]).then_inc(S["ac"], 1)
                prev["kev"] = inc("ac")
                if on("gp"):
                    W("ac", prev["kev"])
                    eng.tensor_reduce(mkac[0:1, 8 * T + hp:8 * T + hp + 1],
                                      kstv[0:128, :], AX.XYZWC, ALU.max,
                                      apply_absolute_value=True).then_inc(S["pl"], 1)
                prev["kgp"] = inc("pl", 1)
                if on("sy"):
                    W("ac", prev["kev"])
                kd1 = dma("sy", ktd[hp, :, 1024 * T:1024 * T + 1024], kstv[0:128, :])
                kd2 = dma("sy", ktd[hp, :, L + 1024 * T:L + 1024 * T + 1024],
                          kstv[0:128, :])
                prev["kdma"] = (kd1, kd2)

            for a in range(8):
                if on("mm") and (T > 0 or a > 0):
                    W("dv", prev["vev"])
                for dh in range(2):
                    for cc in range(8):
                        if on("mm"):
                            eng.matmul(
                                s_b[:, dh, :],
                                xlntv[:, cc, 128 * a:128 * a + 128],
                                wvtv[:, cc, 512 * dh:512 * dh + 512],
                                start=(cc == 0), stop=False)
                    if on("mm"):
                        ins = eng.matmul(s_b[:, dh, :], onesb[:],
                                         bvrb[0:1, 512 * dh:512 * dh + 512],
                                         start=False, stop=True)
                        if dh == 1:
                            ins.then_inc(S["mm"], 1)
                mmv = inc("mm")
                if on("dv"):
                    W("mm", mmv)
                    if T > 0 or a > 0:
                        for t_ in prev["vdma"]:
                            W(t_)
                    if T == 0 and a == 0:
                        eng.memset(vstv[:, :, 64:65], 1.0)
                    eng.tensor_copy(
                        vstv[:, :, 0:64],
                        s_b[:].rearrange("p dh c -> p (dh c)").rearrange(
                            "p (h e) -> p h e", e=64)).then_inc(S["dv"], 1)
                prev["vev"] = inc("dv")
                if on("gp"):
                    W("dv", prev["vev"])
                chunk = 8 * T + a
                vd1 = dma("gp", vsw[chunk, :, :], vst_flat)
                vd2 = dma("gp", vsw[chunk + 64, :, :], vst_flat)
                prev["vdma"] = (vd1, vd2)

            prev["mm_tile"] = cnt["mm"]


        ph1_done = dict(cnt)

        if phases < 2:
            keys = ["xlnt", "kt", "vsw", "p8f", "p8r"] if debug else []
            return finish(me, eng, cnt, S, keys)

        qslv = pool_flat[:, P1_QSL:P1_QSL + 16416].rearrange(
            "p (cc c) -> p cc c", c=2052)
        if on("sy"):
            r = get_rt()
            src = xlntd[:, :, 0:2048]
            src = dataclasses.replace(src, offset=src.offset + r["t0"])
            W(prev["xlnt_dma"])
            eng.dma_start(out=qslv[:, :, 1:2049],
                          in_=src.rearrange("cc p c -> p cc c")).then_inc(S["sx"], 16)
            src0 = xlntd[:, :, 0:1]
            src0 = dataclasses.replace(src0, offset=src0.offset + r["qoff"])
            eng.dma_start(out=qslv[:, :, 0:1],
                          in_=src0.rearrange("cc p c -> p cc c"),
                          allow_slow_non_contiguous=True).then_inc(S["sx"], 16)
        qsl_w = inc("sx", 32)

        qev = 0
        for hp in range(NHP):
            segs = [(0, 512), (512, 512), (1024, 512), (1536, 512), (2048, 1)]
            for s0_, sw in segs:
                if on("mm"):
                    if hp == 0 and s0_ == 0:
                        W("sx", qsl_w)
                        W("ac", ph1_done["ac"])
                    else:
                        W("ac", qev)
                    for cc in range(8):
                        ins = eng.matmul(s_a[:, 0, 0:sw],
                                         wqTb[:, cc, 128 * hp:128 * hp + 128],
                                         qslv[:, cc, s0_:s0_ + sw],
                                         start=(cc == 0), stop=(cc == 7))
                        if cc == 7:
                            ins.then_inc(S["mm"], 1)
                else:
                    pass
                mmq = inc("mm")
                qev = act_copy(qts[:, hp, s0_:s0_ + sw], s_a[:, 0, 0:sw],
                               bias=vecs[:, hp:hp + 1], wait=[("mm", mmq)])
        q_ac = qev
        if SUBPH == 6:
            return finish(me, eng, cnt, S, [])

        if on("ac"):
            W("ac", q_ac)
            eng.activation(atn_b[:, 0:16416],
                           qts[:].rearrange("p a c -> p (a c)"),
                           AF.Square).then_inc(S["ac"], 1)
        qsq_a = inc("ac")
        ncell = 0
        cell_dv = {}
        for hp in range(NHP):
            for sg in range(5):
                w0 = 512 * sg
                sw = 512 if sg < 4 else 1
                bank = spare if ncell % 2 == 0 else bndp
                if on("mm"):
                    if hp == 0 and sg == 0:
                        W("ac", qsq_a)
                    if ncell >= 2:
                        W("dv", cell_dv[ncell - 2])
                    eng.matmul(bank[0:1, 0:sw], onescol[:],
                               atn[:, hp, w0:w0 + sw],
                               start=True, stop=True).then_inc(S["mm"], 1)
                mmn = inc("mm")
                if on("dv"):
                    W("mm", mmn)
                    eng.tensor_reduce(srow2[0:1, ncell:ncell + 1],
                                      bank[0:1, 0:sw], AX.X,
                                      ALU.max).then_inc(S["dv"], 1)
                cell_dv[ncell] = inc("dv")
                ncell += 1
        if on("dv"):
            W("dv", cnt["dv"])
            eng.tensor_reduce(srow[0:1, 1:2], srow2[0:1, 0:ncell], AX.X,
                              ALU.max).then_inc(S["dv"], 1)
        d_q2m = inc("dv")

        if on("dv"):
            W("pl", ph1_done["pl"])
            eng.tensor_reduce(srow[0:1, 0:1], mkac[0:1, 0:64], AX.X,
                              ALU.max).then_inc(S["dv"], 1)
        d_kinf = inc("dv")

        if on("gp"):
            W("ini", 5 * 16)
            eng.tensor_reduce(srow[0:1, 2:3], relb[:], AX.XYZWC, ALU.max,
                              apply_absolute_value=True).then_inc(S["pl"], 1)
        g_mb = inc("pl", 1)
        if on("dv"):
            W("pl", g_mb)
        d_mb = cnt["dv"]

        if SUBPH == 7:
            return finish(me, eng, cnt, S, [])
        if on("ac"):
            W("dv", max(d_q2m, d_kinf, d_mb))
            eng.activation(srow[0:1, 3:4], srow[0:1, 1:2], AF.Ln).then_inc(S["ac"], 1)
        a_ln = inc("ac")
        if on("ac"):
            W("ac", a_ln)
            eng.activation(srow[0:1, 4:5], srow[0:1, 3:4], AF.Exp,
                           scale=0.5).then_inc(S["ac"], 1)
        a_sq = inc("ac")
        if on("dv"):
            W("ac", a_sq)
            eng.tensor_tensor(srow[0:1, 5:6], srow[0:1, 4:5], srow[0:1, 0:1],
                              op=ALU.mult).then_inc(S["dv"], 1)
        d_c1 = inc("dv")
        if on("dv"):
            W("dv", d_c1)
            eng.scalar_tensor_tensor(srow[0:1, 6:7], srow[0:1, 2:3], 2.0,
                                     srow[0:1, 5:6], op0=ALU.mult,
                                     op1=ALU.add).then_inc(S["dv"], 1)
        d_c = inc("dv")
        if on("dv"):
            W("dv", d_c)
            eng.tensor_scalar(srow[0:1, 7:8], srow[0:1, 6:7], -1.0, None,
                              op0=ALU.mult).then_inc(S["dv"], 1)
        d_negc = inc("dv")

        if on("mm"):
            W("dv", d_negc)
            eng.matmul(spare[0:16, 0:32], relbb[:], identb[0:32, 0:32],
                       start=True, stop=True).then_inc(S["mm"], 1)
        t_rt = inc("mm")
        rt_ev = act_copy(stats[0:16, 0:32], spare[0:16, 0:32],
                         wait=[("mm", t_rt)])
        if on("dv"):
            W("ac", rt_ev)
            eng.tensor_tensor(colv[0:16, 2:3], stats[0:16, 31:32],
                              stats[0:16, 0:1], op=ALU.subtract).then_inc(S["dv"], 1)
        d_b31c = inc("dv")
        mmt, rowap = part_row(colv[:, 2:3], 2, d_b31c)
        if on("dv"):
            W("mm", mmt)
            eng.tensor_copy(srow[0:1, 16:32], rowap[0:1, 0:16]).then_inc(S["dv"], 1)
        d_b31r = inc("dv")
        if on("dv"):
            W("dv", d_b31r)
            eng.tensor_tensor(srow[0:1, 32:40], srow[0:1, 17:33:2],
                              srow[0:1, 16:32:2], op=ALU.subtract).then_inc(S["dv"], 1)
        d_d31 = inc("dv")

        if SUBPH == 8:
            return finish(me, eng, cnt, S, [])
        prev_tab_ev = 0
        for hp in range(NHP):
            if on("dv"):
                if hp == 0:
                    W("dv", d_d31)
                    W("dv", d_negc)
                else:
                    W("ac", prev_tab_ev)
                eng.tensor_scalar(srow2[0:1, 0:256], clsm[:],
                                  srow[0:1, 16 + 2 * hp:17 + 2 * hp],
                                  srow[0:1, 7:8], op0=ALU.mult,
                                  op1=ALU.add).then_inc(S["dv"], 1)
            d_row = inc("dv")
            if on("dv"):
                W("dv", d_row)
                eng.tensor_copy(bbldb[0:1, 0:256],
                                srow2[0:1, 0:256]).then_inc(S["dv"], 1)
            d_hi = inc("dv")
            if on("dv"):
                W("dv", d_hi)
                eng.tensor_copy(bbld[0:1, 0:256], bbldb[0:1, 0:256]).then_inc(S["dv"], 1)
            d_hif = inc("dv")
            if on("dv"):
                W("dv", d_hif)
                eng.tensor_tensor(bbld[0:1, 256:512], srow2[0:1, 0:256],
                                  bbld[0:1, 0:256], op=ALU.subtract).then_inc(S["dv"], 1)
            d_lo = inc("dv")
            if on("dv"):
                W("dv", d_lo)
                eng.tensor_copy(bbldb[0:1, 256:512],
                                bbld[0:1, 256:512]).then_inc(S["dv"], 1)
            d_lob = inc("dv")
            for sg in range(2):
                if on("mm"):
                    W("dv", d_lob)
                    if prev_tab_ev:
                        W("ac", prev_tab_ev)
                    eng.matmul(spare[:, 128 * sg:128 * sg + 128], onesb[:],
                               bbldb[0:1, 128 * sg:128 * sg + 128],
                               start=True, stop=False)
                    eng.matmul(spare[:, 128 * sg:128 * sg + 128], onesb[:],
                               bbldb[0:1, 256 + 128 * sg:256 + 128 * sg + 128],
                               start=False, stop=True).then_inc(S["mm"], 1)
                mmb = inc("mm")
                prev_tab_ev = act_copy(
                    vtab[:, 256 * hp + 128 * sg:256 * hp + 128 * sg + 128],
                    spare[:, 128 * sg:128 * sg + 128], wait=[("mm", mmb)])
            if on("dv"):
                W("ac", prev_tab_ev)
                eng.tensor_scalar(srow2[0:1, 0:256], clsm[:],
                                  srow[0:1, 17 + 2 * hp:18 + 2 * hp],
                                  srow[0:1, 7:8], op0=ALU.mult,
                                  op1=ALU.add).then_inc(S["dv"], 1)
            d_row1 = inc("dv")
            if on("dv"):
                W("dv", d_row1)
                eng.tensor_copy(bbldb[0:1, 0:256],
                                srow2[0:1, 0:256]).then_inc(S["dv"], 1)
            d_hi1 = inc("dv")
            if on("dv"):
                W("dv", d_hi1)
                eng.tensor_copy(bbld[0:1, 0:256], bbldb[0:1, 0:256]).then_inc(S["dv"], 1)
            d_hif1 = inc("dv")
            if on("dv"):
                W("dv", d_hif1)
                eng.tensor_tensor(bbld[0:1, 256:512], srow2[0:1, 0:256],
                                  bbld[0:1, 0:256], op=ALU.subtract).then_inc(S["dv"], 1)
            d_lo1 = inc("dv")
            if on("dv"):
                W("dv", d_lo1)
                eng.tensor_copy(bbldb[0:1, 256:512],
                                bbld[0:1, 256:512]).then_inc(S["dv"], 1)
            d_lob1 = inc("dv")
            for sg in range(2):
                if on("mm"):
                    W("dv", d_lob1)
                    W("ac", prev_tab_ev)
                    eng.matmul(spare[:, 256 + 128 * sg:384 + 128 * sg], onesb[:],
                               bbldb[0:1, 128 * sg:128 * sg + 128],
                               start=True, stop=False)
                    eng.matmul(spare[:, 256 + 128 * sg:384 + 128 * sg], onesb[:],
                               bbldb[0:1, 256 + 128 * sg:256 + 128 * sg + 128],
                               start=False, stop=True).then_inc(S["mm"], 1)
                mmb = inc("mm")
                prev_tab_ev = act_copy(
                    vtab1[:, 256 * hp + 128 * sg:256 * hp + 128 * sg + 128],
                    spare[:, 256 + 128 * sg:384 + 128 * sg], wait=[("mm", mmb)])

        if phases < 3:
            keys = (["xlnt", "kt", "vsw", "p8f", "p8r", "qt", "sc", "vtab",
                     "vtab1"] if debug else [])
            return finish(me, eng, cnt, S, keys)

        ktv = pool_flat[:, P2_KT:P2_KT + 16384].rearrange(
            "p (b k) -> p b k", k=8192)
        vbv = pool_flat[:, P2_V:P2_V + 16640].rearrange(
            "p (b k e) -> p b k e", k=64, e=130)
        b8v = pf32[:, 16512:22656].rearrange(
            "p (s h q) -> p s h q", h=2, q=512)
        rbB2 = pf32[:, 22656:23680]
        bbld2 = pf32[0:2, 23680:24704]
        bbldb2 = pool_flat[0:2, 49408:50432]
        stg = pool_flat[0:64, 50432:51458].rearrange("p (h q) -> p h q", q=513)
        rb64h = {
            (0, "hi"): pool_flat[64:65, 51464:51976],
            (0, "lo"): pool_flat[64:65, 51976:52488],
            (1, "hi"): pool_flat[64:65, 52488:53000],
            (1, "lo"): pool_flat[64:65, 53000:53512],
        }
        rbbnd = pool_flat[64:65, 53512:53516]
        rb_f_tmp = pf32[64:65, 26760:27272]
        rb_f_tmp2 = pf32[64:65, 27272:27784]

        act_id = {}
        prev2 = {"b8dv": 0, "pvdone": 0, "stgdma": [], "bndexp": 0,
                 "bnd_norm": 0, "bcast_done": 0, "normP": {0: 0, 1: 0},
                 "bbcast": cnt["dv"], "norm_dv": 0,
                 "kvfree": {0: 0, 1: 0}}
        ph2_ac0 = cnt["ac"]
        ph2_dv0 = cnt["dv"]
        ph2_mm0 = cnt["mm"]

        ph2_slot0 = {k: 16 * v for k, v in slot_uses.items() if v > 0}
        ph2_ini0 = cnt.get("ini", 0)
        ph2_sx0 = cnt.get("sx", 0)

        def ph1_drain():
            W("mm", ph2_mm0)
            W("ac", ph2_ac0)
            W("dv", ph2_dv0)
            for k, v in ph2_slot0.items():
                W(k, v)
            if ph2_ini0:
                W("ini", ph2_ini0)
            if ph2_sx0:
                W("sx", ph2_sx0)

        def rt_dma(queue, out, in_ap, off_fn, pre=None):
            name = f"{queue}{dma_rr[queue] % NSLOT}"
            dma_rr[queue] += 1
            if on(queue):
                r = get_rt()
                if slot_uses[name] > 0:
                    W(name, 16 * slot_uses[name])
                if pre is not None:
                    pre()
                srcx = dataclasses.replace(
                    in_ap, offset=in_ap.offset + off_fn(r))
                eng.dma_start(out=out, in_=srcx).then_inc(S[name], 16)
            slot_uses[name] += 1
            cnt[name] = 16 * slot_uses[name]
            return (name, 16 * slot_uses[name])

        def load_kv(hp):
            dbuf = hp % 2
            tl = []
            if on("sy"):
                if hp == 0:
                    ph1_drain()
                W("mm", prev2["kvfree"][dbuf])
            ktw = rt_dma("sy", ktv[:, dbuf, :], ktd[hp, :, 0:8192],
                         lambda r: r["j"] * 2048)
            tl.append(ktw)
            vw = rt_dma(
                "sy", vbv[:, dbuf, :, :],
                vsw[0:64, :, 130 * hp:130 * hp + 130].rearrange("k p c -> p k c"),
                lambda r: r["j"] * (16 * 128 * 1040))
            tl.append(vw)
            return tl

        def load_hp(hp, kv_pre):
            tl = list(kv_pre) if kv_pre is not None else load_kv(hp)
            for h in range(2):
                row = 16384 * (2 * hp + h)
                for c2 in range(4):
                    if on("gp") and h == 0 and c2 == 0:
                        if hp == 0:
                            ph1_drain()
                        W("dv", prev2["b8dv"])
                    srcap = dataclasses.replace(
                        p8f[0:1, 0:512],
                        offset=p8f.offset + row + 7680 + 128 * c2,
                        ap=[[1, 128], [1, 512]])
                    tl.append(dma("gp", b8v[:, c2, h, :], srcap))
                srcap = dataclasses.replace(
                    p8f[0:1, 0:512], offset=p8f.offset + row + 7552,
                    ap=[[1, 128], [1, 512]])
                tl.append(dma("gp", b8v[:, 5, h, :], srcap))
                base4 = dataclasses.replace(
                    p8f[0:1, 0:512], offset=p8f.offset + row + 7552,
                    ap=[[1, 128], [1, 512]])
                tl.append(rt_dma("gp", b8v[:, 4, h, :], base4,
                                 lambda r: (1 - (r["j"] >= 1)) * 8192))
            return tl

        def build_bb(hp):
            lastmm = 0
            cb = prev2.get("bb_lastcb", 0)
            for sblk in range(8):
                if on("gp"):
                    W("dv", cb)
                    if sblk == 0:
                        W("ac", prev2.get("bnd_sp_ac", 0))
                w = dma("gp", bbld2[0:2, 0:1024],
                        p8r[2 * hp:2 * hp + 2, 1024 * sblk:1024 * sblk + 1024])
                if on("dv"):
                    W(w)
                    W("mm", lastmm)
                    eng.tensor_copy(bbldb2[0:2, 0:1024],
                                    bbld2[0:2, 0:1024]).then_inc(S["dv"], 1)
                cb = inc("dv")
                if on("mm"):
                    W("dv", cb)
                    if sblk == 0:
                        W("dv", prev2["bbcast"])
                        W("dv", prev2["norm_dv"])
                        W("ac", prev2.get("bnd_sp_ac", 0))
                    for b in range(8):
                        ins = eng.matmul(
                            spare[:, 16 * sblk + 2 * b:16 * sblk + 2 * b + 2],
                            bbldb2[0:2, 128 * b:128 * b + 128],
                            identb[0:2, 0:2], start=True, stop=True)
                        if b == 7:
                            ins.then_inc(S["mm"], 1)
                lastmm = inc("mm")
            if on("dv"):
                W("mm", lastmm)
                eng.tensor_copy(bb_st[:], spare[:, 0:128]).then_inc(S["dv"], 1)
            prev2["bbcast"] = inc("dv")
            prev2["bb_lastcb"] = cb
            return prev2["bbcast"]

        kv_pending = None
        for hp in range(min(NHP, PH2N)):
            dbuf = hp % 2
            kvw = load_hp(hp, kv_pending)
            kv_pending = None
            if PH2M < 1:
                continue
            bbdv = build_bb(hp)
            if PH2M < 2:
                continue
            for qt in range(NQT):
                qcol0 = 1 + 512 * qt
                DEP = 2
                pvp = ((pv0, pv1) if qt % 2 == 0
                       else (bndp[0:65, 0:512], spare[0:65, 0:512]))
                for c2 in range(NCH):
                    cstar = (c2 + 4 * qt) % 64
                    pair = ((s_a[:, 0, :], s_a[:, 1, :]),
                            (s_b[:, 0, :], s_b[:, 1, :]))[c2 % 2]
                    near = c2 in NEAR
                    b8slot = c2 if c2 < 4 else (4 if qt == 0 else 5)
                    colidx = hp * 256 + qt * 64 + c2
                    def bnd_pair(cst_):
                        eng.matmul(bndp[:, c2:c2 + 1],
                                   ktv[0:64, dbuf, 128 * cst_:128 * cst_ + 128],
                                   qts[0:64, hp, 0:1], start=True, stop=True)
                        eng.matmul(spare[:, c2:c2 + 1],
                                   ktv[64:128, dbuf,
                                       128 * cst_:128 * cst_ + 128],
                                   qts[64:128, hp, 0:1], start=True,
                                   stop=True, tile_position=(64, 0))
                    if on("mm"):
                        if c2 >= DEP:
                            W("ac", act_id[(hp, qt, c2 - DEP)])
                        elif qt > 0:
                            W("ac", act_id[(hp, qt - 1, NCH - DEP + c2)])
                            if c2 < 2:
                                W("dv", prev2["bcast_done"])
                        elif hp > 0:
                            W("ac", act_id[(hp - 1, NQT - 1, NCH - DEP + c2)])
                            if c2 < 2:
                                W("dv", prev2["bcast_done"])
                        else:
                            W("ac", ph2_ac0)
                            W("dv", ph2_dv0)
                        if c2 == 0 and qt == 0:
                            for t_ in kvw[:2]:
                                W(t_)
                        if qt == 0 and c2 == 0 and (PH2X & 2):
                            W("dv", bbdv)
                            bnd_pair(cstar)
                        eng.matmul(pair[0],
                                   ktv[0:64, dbuf, 128 * cstar:128 * cstar + 128],
                                   qts[0:64, hp, qcol0:qcol0 + 512],
                                   start=True, stop=True)
                        eng.matmul(pair[1],
                                   ktv[64:128, dbuf, 128 * cstar:128 * cstar + 128],
                                   qts[64:128, hp, qcol0:qcol0 + 512],
                                   start=True, stop=True,
                                   tile_position=(64, 0)).then_inc(S["mm"], 1)
                    s_inc = inc("mm")
                    if near and not (PH2X & 1):
                        near = False
                    if near:
                        if on("dv"):
                            W("mm", s_inc)
                            if qt == 0 and c2 == 0:
                                for t_ in kvw[2:]:
                                    W(t_)
                            def b8rev(sl, hh):
                                bb_ = b8v[:, sl, hh, :]
                                return dataclasses.replace(
                                    bb_, ap=[[bb_.ap[0][0], 128], [-1, 512]],
                                    offset=bb_.offset + 511)
                            eng.tensor_tensor(pair[0], pair[0],
                                              b8rev(b8slot, 0), op=ALU.add)
                            eng.tensor_tensor(pair[1], pair[1],
                                              b8rev(b8slot, 1),
                                              op=ALU.add).then_inc(S["dv"], 1)
                        b8dv = inc("dv")
                        if qt == NQT - 1 and c2 == NCH - 1:
                            prev2["b8dv"] = b8dv
                    else:
                        b8dv = None
                    if on("ac"):
                        W("mm", s_inc)
                        if b8dv is not None:
                            W("dv", b8dv)
                        eng.activation(ptb[:, c2 % 2, 0, :], pair[0],
                                       AF.Exp, bias=vtab[:, colidx:colidx + 1],
                                       scale=0.125)
                        eng.activation(ptb[:, c2 % 2, 1, :], pair[1],
                                       AF.Exp, bias=vtab1[:, colidx:colidx + 1],
                                       scale=0.125).then_inc(S["ac"], 1)
                    act_id[(hp, qt, c2)] = inc("ac")
                    if c2 >= 1 and (PH2X & 4):
                        cprev = c2 - 1
                        cstar_p = (cprev + 4 * qt) % 64
                        if on("mm"):
                            W("ac", act_id[(hp, qt, cprev)])
                            if cprev == 0:
                                W("dv", prev2["normP"][qt % 2])
                                if qt == 1:
                                    W("ac", prev2["bndexp"])
                                    W("dv", prev2["bnd_norm"])
                            eng.matmul(pvp[0][:], vbv[:, dbuf, cstar_p, 0:65],
                                       ptb[:, cprev % 2, 0, :],
                                       start=(cprev == 0), stop=False)
                            eng.matmul(pvp[1][:], vbv[:, dbuf, cstar_p, 65:130],
                                       ptb[:, cprev % 2, 1, :],
                                       start=(cprev == 0), stop=False)
                            if qt == 0 and (PH2X & 2):
                                bnd_pair(cstar)
                    elif c2 >= 1 and qt == 0 and (PH2X & 2):
                        if on("mm"):
                            bnd_pair(cstar)
                cprev = NCH - 1
                cstar_p = (cprev + 4 * qt) % 64
                if on("mm") and (PH2X & 4):
                    W("ac", act_id[(hp, qt, cprev)])
                    eng.matmul(pvp[0][:], vbv[:, dbuf, cstar_p, 0:65],
                               ptb[:, cprev % 2, 0, :], start=False, stop=True)
                    eng.matmul(pvp[1][:], vbv[:, dbuf, cstar_p, 65:130],
                               ptb[:, cprev % 2, 1, :], start=False,
                               stop=True).then_inc(S["mm"], 1)
                pv_done = inc("mm") if (PH2X & 4) else cnt["mm"]
                if qt == NQT - 1:
                    prev2["kvfree"][dbuf] = pv_done
                    prev2["bnd_sp_ac"] = act_id[(hp, qt, NCH - 1)]

                if PH2M < 3:
                    continue
                bnd_norm = 0
                if qt == 0:
                    bbsl = bb_st[:, 0:64]
                    ebb0 = dataclasses.replace(
                        bbsl, ap=[[bbsl.ap[0][0], 128], [2, 64]])
                    ebb1 = dataclasses.replace(
                        bbsl, ap=[[bbsl.ap[0][0], 128], [2, 64]],
                        offset=bbsl.offset + 1)
                    if on("dv"):
                        W("mm", pv_done)
                        W("dv", bbdv)
                        eng.tensor_tensor(bndp[:, 0:64], bndp[:, 0:64],
                                          ebb0, op=ALU.add)
                        eng.tensor_tensor(spare[:, 0:64], spare[:, 0:64],
                                          ebb1, op=ALU.add).then_inc(S["dv"], 1)
                    bbadd = inc("dv")
                    if on("ac"):
                        W("dv", bbadd)
                        eng.activation(ptbd[:, 0:64], bndp[:, 0:64], AF.Exp,
                                       bias=vtab[:, 256 * hp:256 * hp + 1],
                                       scale=0.125)
                        eng.activation(ptbd[:, 64:128], spare[:, 0:64], AF.Exp,
                                       bias=vtab1[:, 256 * hp:256 * hp + 1],
                                       scale=0.125).then_inc(S["ac"], 1)
                    bndexp = inc("ac")
                    prev2["bnd_sp_ac"] = bndexp
                    prev2["bndexp"] = bndexp
                    if on("mm"):
                        W("ac", bndexp)
                        for h in range(2):
                            for c2b in range(NCH):
                                cst = c2b % 64
                                ins = eng.matmul(
                                    bndp[0:65, 128 + h:129 + h],
                                    vbv[:, dbuf, cst, 65 * h:65 * h + 65],
                                    ptbd[:, 64 * h + c2b:64 * h + c2b + 1],
                                    start=(c2b == 0), stop=(c2b == NCH - 1))
                        ins.then_inc(S["mm"], 1)
                    bndpv = inc("mm")
                    if on("dv"):
                        W("mm", bndpv)
                        for t_ in prev2["stgdma"]:
                            W(t_)
                        eng.reciprocal(rbB2[64:65, 1020:1022],
                                       bndp[64:65, 128:130]).then_inc(S["dv"], 1)
                    r1 = inc("dv")
                    if on("dv"):
                        W("dv", r1)
                        eng.tensor_copy(rbbnd[0:1, 0:2], rbB2[64:65, 1020:1022]).then_inc(S["dv"], 1)
                    r2 = inc("dv")
                    if on("dv"):
                        W("dv", r2)
                        eng.tensor_copy(rb_f_tmp[0:1, 0:2], rbbnd[0:1, 0:2]).then_inc(S["dv"], 1)
                    r2b = inc("dv")
                    if on("dv"):
                        W("dv", r2b)
                        eng.tensor_tensor(rb_f_tmp2[0:1, 0:2], rbB2[64:65, 1020:1022],
                                          rb_f_tmp[0:1, 0:2],
                                          op=ALU.subtract).then_inc(S["dv"], 1)
                    r3 = inc("dv")
                    if on("dv"):
                        W("dv", r3)
                        eng.tensor_copy(rbbnd[0:1, 2:4],
                                        rb_f_tmp2[0:1, 0:2]).then_inc(S["dv"], 1)
                    r4 = inc("dv")
                    if on("mm"):
                        W("dv", r4)
                        W("ac", act_id[(hp, 0, NCH - 2)])
                        eng.matmul(s_a[0:64, 0, 0:2], onesw[64:65, 0:64],
                                   rbbnd[0:1, 0:2], start=True, stop=False)
                        eng.matmul(s_a[0:64, 0, 0:2], onesw[64:65, 0:64],
                                   rbbnd[0:1, 2:4], start=False,
                                   stop=True).then_inc(S["mm"], 1)
                    bb_bc = inc("mm")
                    if on("dv"):
                        W("mm", bb_bc)
                        eng.tensor_copy(rbB2[0:64, 1022:1024],
                                        s_a[0:64, 0, 0:2]).then_inc(S["dv"], 1)
                    bb_bc2 = inc("dv")
                    for h in range(2):
                        if on("dv"):
                            W("dv", bb_bc2)
                            ins = eng.tensor_tensor(stg[:, h, 0:1],
                                                    bndp[0:64, 128 + h:129 + h],
                                                    rbB2[0:64, 1022 + h:1023 + h],
                                                    op=ALU.mult)
                            if h == 1:
                                ins.then_inc(S["dv"], 1)
                    bnd_norm = inc("dv")
                    prev2["bnd_norm"] = bnd_norm

                if PH2M < 4:
                    continue
                if on("dv"):
                    W("mm", pv_done)
                    for t_ in prev2["stgdma"]:
                        W(t_)
                    eng.reciprocal(rbB2[64:65, 0:512], pvp[0][64:65, :])
                    eng.reciprocal(rbB2[64:65, 512:1024],
                                   pvp[1][64:65, :]).then_inc(S["dv"], 1)
                n1 = inc("dv")
                if on("dv"):
                    W("dv", n1)
                    eng.tensor_copy(rb64h[(0, "hi")], rbB2[64:65, 0:512])
                    eng.tensor_copy(rb64h[(1, "hi")],
                                    rbB2[64:65, 512:1024]).then_inc(S["dv"], 1)
                n2 = inc("dv")
                if on("dv"):
                    W("dv", n2)
                    eng.tensor_copy(rb_f_tmp[0:1, 0:512], rb64h[(0, "hi")])
                    eng.tensor_copy(rb_f_tmp2[0:1, 0:512],
                                    rb64h[(1, "hi")]).then_inc(S["dv"], 1)
                n3 = inc("dv")
                if on("dv"):
                    W("dv", n3)
                    eng.tensor_tensor(rb_f_tmp[0:1, 0:512], rbB2[64:65, 0:512],
                                      rb_f_tmp[0:1, 0:512], op=ALU.subtract)
                    eng.tensor_tensor(rb_f_tmp2[0:1, 0:512], rbB2[64:65, 512:1024],
                                      rb_f_tmp2[0:1, 0:512],
                                      op=ALU.subtract).then_inc(S["dv"], 1)
                n4 = inc("dv")
                if on("dv"):
                    W("dv", n4)
                    eng.tensor_copy(rb64h[(0, "lo")], rb_f_tmp[0:1, 0:512])
                    eng.tensor_copy(rb64h[(1, "lo")],
                                    rb_f_tmp2[0:1, 0:512]).then_inc(S["dv"], 1)
                n5 = inc("dv")
                if on("mm"):
                    W("dv", n5)
                    W("ac", act_id[(hp, qt, NCH - 1)])
                    eng.matmul(s_a[0:64, 0, :], onesw[64:65, 0:64],
                               rb64h[(0, "hi")], start=True, stop=False)
                    eng.matmul(s_a[0:64, 0, :], onesw[64:65, 0:64],
                               rb64h[(0, "lo")], start=False, stop=True)
                    eng.matmul(s_b[0:64, 0, :], onesw[64:65, 0:64],
                               rb64h[(1, "hi")], start=True, stop=False)
                    eng.matmul(s_b[0:64, 0, :], onesw[64:65, 0:64],
                               rb64h[(1, "lo")], start=False,
                               stop=True).then_inc(S["mm"], 1)
                n_bc = inc("mm")
                if on("dv"):
                    W("mm", n_bc)
                    eng.tensor_copy(rbB2[0:64, 0:512], s_a[0:64, 0, :])
                    eng.tensor_copy(rbB2[0:64, 512:1024],
                                    s_b[0:64, 0, :]).then_inc(S["dv"], 1)
                n_bc2 = inc("dv")
                prev2["bcast_done"] = n_bc2
                for h in range(2):
                    pvh = pvp[h]
                    if on("dv"):
                        W("dv", n_bc2)
                        ins = eng.tensor_tensor(
                            stg[:, h, 1:513], pvh[0:64, :],
                            rbB2[0:64, 512 * h:512 * h + 512], op=ALU.mult)
                        if h == 1:
                            ins.then_inc(S["dv"], 1)
                norm_dv = inc("dv")
                prev2["norm_dv"] = norm_dv
                prev2["normP"][qt % 2] = norm_dv
                c0 = 0 if qt == 0 else 1
                dmas = []
                for h in range(2):
                    if on("gp"):
                        W("dv", norm_dv)
                        if qt == 0:
                            W("dv", bnd_norm)
                    dmas.append(dma("gp",
                                    atn[64 * h:64 * h + 64, hp,
                                        qcol0 - 1 + c0:qcol0 + 512],
                                    stg[:, h, c0:513]))
                prev2["stgdma"] = dmas
                if qt == 0 and hp + 1 < min(NHP, PH2N):
                    kv_pending = load_kv(hp + 1)

        ph2_done = dict(cnt)
        if phases < 4:
            keys = (["atn", "sc", "kt", "vsw", "qt", "p8f", "vtab", "vtab1"]
                    if debug else [])
            return finish(me, eng, cnt, S, keys)

        ph2_fin = {"mm": cnt["mm"], "ac": cnt["ac"], "dv": cnt["dv"]}
        ph2_slots = {k: 16 * v for k, v in slot_uses.items() if v > 0}

        def ph2_drain():
            W("mm", ph2_fin["mm"])
            W("ac", ph2_fin["ac"])
            W("dv", ph2_fin["dv"])
            for k_, v_ in ph2_slots.items():
                W(k_, v_)

        w3v = pool_flat[:, 0:32768].rearrange("p (w cc d) -> p w cc d",
                                              cc=8, d=1024)
        rfv = pf32[:, 16384:20480].rearrange("p (dd c) -> p dd c", c=512)
        rff = pf32[:, 16384:20480]
        xrv = pool_flat[:, P3_XR:P3_XR + 4096].rearrange(
            "p (cc c) -> p cc c", c=512)
        xkv = pool_flat[:, P3_XK:P3_XK + 4096].rearrange(
            "p (cc c) -> p cc c", c=512)
        tm4 = pf32[:, 24576:28672].rearrange("p (a c) -> p a c", c=1024)
        tm8 = pf32[:, 24576:28672].rearrange("p (a c) -> p a c", c=512)
        ptbF = ptb[:].rearrange("p a b c -> p (a b c)")
        atnF = atn[:].rearrange("p a c -> p (a c)").bitcast(F32)
        x2rd = atnF[:, 0:4096]
        banksl = [s_a[:, 0, :], s_a[:, 1, :], s_b[:, 0, :], s_b[:, 1, :]]

        w3l = []
        for wi in range(4):
            if on("gp") and wi == 0:
                ph2_drain()
            w3l.append(dma("gp", pool_flat[:, 8192 * wi:8192 * wi + 8192],
                           w3d[wi, :, :, :].rearrange("p cc d -> p (cc d)")))

        p3 = {"opev": {0: 0, 1: 0, 2: 0, 3: 0}, "btev": 0,
              "xfree": {s: (0, 0, None) for s in range(4)},
              "hilo": 0, "tp_last": 0, "bankrd": 0, "x2slot": None,
              "ln2_last": 0, "acc2_last": 0}

        FEN = {"dump": 0}

        def ln_psum(sc0, tp_wait):
            if on("ac"):
                W("mm", tp_wait)
                W("ac", FEN["dump"])
                eng.activation(ptbF[:, 1024:1536], bndp[:, 0:512], AF.Identity,
                               accum_out=sums[:, 0, 0:1])
                eng.activation(ptbF[:, 1536:2048], spare[:, 0:512], AF.Identity,
                               accum_out=sums[:, 1, 0:1])
                eng.activation(ptbF[:, 0:512], bndp[:, 0:512], AF.Square,
                               accum_out=sums[:, 0, 1:2])
                eng.activation(ptbF[:, 512:1024], spare[:, 0:512], AF.Square,
                               accum_out=sums[:, 1, 1:2]).then_inc(S["ac"], 1)
            acc = inc("ac")
            FEN["dump"] = acc
            if on("dv"):
                W("ac", acc)
                eng.tensor_tensor(stats[:, sc0:sc0 + 2], sums[:, 0, 0:2],
                                  sums[:, 1, 0:2], op=ALU.add).then_inc(S["dv"], 1)
            d1 = inc("dv")
            if on("dv"):
                W("dv", d1)
                eng.tensor_scalar(stats[:, sc0 + 2:sc0 + 3],
                                  stats[:, sc0:sc0 + 1], -1.0 / 1024, None,
                                  op0=ALU.mult)
                eng.tensor_scalar(stats[:, sc0 + 3:sc0 + 4],
                                  stats[:, sc0 + 1:sc0 + 2], 1.0 / 1024, None,
                                  op0=ALU.mult).then_inc(S["dv"], 1)
            d2 = inc("dv")
            if on("dv"):
                W("dv", d2)
                eng.tensor_tensor(stats[:, sc0 + 4:sc0 + 5],
                                  stats[:, sc0 + 2:sc0 + 3],
                                  stats[:, sc0 + 2:sc0 + 3],
                                  op=ALU.mult).then_inc(S["dv"], 1)
            d3 = inc("dv")
            if on("dv"):
                W("dv", d3)
                eng.scalar_tensor_tensor(stats[:, sc0 + 3:sc0 + 4],
                                         stats[:, sc0 + 3:sc0 + 4], 1e-5,
                                         stats[:, sc0 + 4:sc0 + 5],
                                         op0=ALU.add,
                                         op1=ALU.subtract).then_inc(S["dv"], 1)
            d4 = inc("dv")
            if on("ac"):
                W("dv", d4)
                eng.activation(stats[:, sc0 + 5:sc0 + 6],
                               stats[:, sc0 + 3:sc0 + 4], AF.Ln).then_inc(S["ac"], 1)
            a1 = inc("ac")
            if on("ac"):
                W("ac", a1)
                eng.activation(stats[:, sc0 + 6:sc0 + 7],
                               stats[:, sc0 + 5:sc0 + 6], AF.Exp,
                               scale=-0.5).then_inc(S["ac"], 1)
            a2 = inc("ac")
            if on("dv"):
                W("ac", a2)
                eng.tensor_tensor(stats[:, sc0 + 7:sc0 + 8],
                                  stats[:, sc0 + 2:sc0 + 3],
                                  stats[:, sc0 + 6:sc0 + 7],
                                  op=ALU.mult).then_inc(S["dv"], 1)
            d5 = inc("dv")
            if on("dv"):
                W("dv", d5)
                eng.tensor_scalar(bndp[:, 0:512], bndp[:, 0:512],
                                  stats[:, sc0 + 6:sc0 + 7],
                                  stats[:, sc0 + 7:sc0 + 8], op0=ALU.mult,
                                  op1=ALU.add)
                eng.tensor_scalar(spare[:, 0:512], spare[:, 0:512],
                                  stats[:, sc0 + 6:sc0 + 7],
                                  stats[:, sc0 + 7:sc0 + 8], op0=ALU.mult,
                                  op1=ALU.add).then_inc(S["dv"], 1)
            return inc("dv")

        for t3 in range(4):
            C0 = 512 * t3
            for dd in range(8):
                bank = banksl[dd % 4]
                if on("mm"):
                    if t3 == 0 and dd == 0:
                        ph2_drain()
                        for t_ in w3l[:1]:
                            W(t_)
                    if dd >= 4:
                        W("ac", p3["opev"][dd % 4])
                    elif t3 > 0 or dd >= 4:
                        W("ac", p3["opev"][dd % 4])
                    if dd == 0:
                        W("ac", p3["btev"])
                    ins = None
                    for cc in range(8):
                        ins = eng.matmul(bank, w3v[:, 0, cc, 128 * dd:128 * dd + 128],
                                         atn[:, cc, C0:C0 + 512],
                                         start=(cc == 0), stop=(cc == 7))
                    ins.then_inc(S["mm"], 1)
                mmid = inc("mm")
                if on("ac"):
                    W("mm", mmid)
                    if t3 > 0 and dd == 0:
                        W("dv", p3["hilo"])
                    eng.activation(rfv[:, dd, :], bank, AF.Identity,
                                   bias=vecs[:, 24 + dd:25 + dd]).then_inc(S["ac"], 1)
                p3["opev"][dd % 4] = inc("ac")
            opev7 = p3["opev"][3]
            if on("dv"):
                for b_ in range(4):
                    W("ac", p3["opev"][b_])
                if t3 > 0:
                    W("mm", p3["tp_last"])
                eng.tensor_copy(xrv[:, :, :], rfv[:, :, :]).then_inc(S["dv"], 1)
            hi_a = inc("dv")
            if on("dv"):
                W("dv", hi_a)
                eng.scalar_tensor_tensor(xkv[:, :, :], xrv[:, :, :], -1.0,
                                         rfv[:, :, :], op0=ALU.mult,
                                         op1=ALU.add).then_inc(S["dv"], 1)
            p3["hilo"] = inc("dv")

            for s3 in range(4):
                ac_f, dv_f, sl_f = p3["xfree"][s3]
                if on("gp"):
                    W("ac", ac_f)
                    W("dv", dv_f)
                    if sl_f is not None:
                        W(sl_f)
                wx = dma("gp", tm4[:, s3, :],
                         xsd[C0 + 128 * s3:C0 + 128 * s3 + 128, :])
                if on("mm"):
                    W("dv", p3["hilo"])
                    W("dv", p3["bankrd"])
                    ins = None
                    for dd in range(8):
                        bank = bndp if dd < 4 else spare
                        col = 128 * (dd % 4)
                        eng.matmul(bank[:, col:col + 128],
                                   xrv[:, dd, 128 * s3:128 * s3 + 128],
                                   identb[:], start=True, stop=False)
                        ins = eng.matmul(bank[:, col:col + 128],
                                         xkv[:, dd, 128 * s3:128 * s3 + 128],
                                         identb[:], start=False, stop=True)
                    ins.then_inc(S["mm"], 1)
                tp = inc("mm")
                p3["tp_last"] = tp
                lnp = ln_psum(16, tp)
                if on("dv"):
                    W("dv", lnp)
                    W(wx)
                    eng.tensor_tensor(tm4[:, s3, 0:512], bndp[:, 0:512],
                                      tm4[:, s3, 0:512], op=ALU.add)
                    eng.tensor_tensor(tm4[:, s3, 512:1024], spare[:, 0:512],
                                      tm4[:, s3, 512:1024],
                                      op=ALU.add).then_inc(S["dv"], 1)
                x2id = inc("dv")
                p3["bankrd"] = x2id
                if on("gp"):
                    W("dv", x2id)
                wx2 = dma("gp", x2d[C0 + 128 * s3:C0 + 128 * s3 + 128, :],
                          tm4[:, s3, :])
                p3["x2slot"] = wx2
                if on("ac"):
                    W("dv", x2id)
                    eng.activation(ptbF[:, 1024:2048], tm4[:, s3, :], AF.Identity,
                                   accum_out=sums[:, 2, 0:1])
                    eng.activation(ptbF[:, 0:1024], tm4[:, s3, :], AF.Square,
                                   accum_out=sums[:, 2, 1:2]).then_inc(S["ac"], 1)
                acc2 = inc("ac")
                FEN["dump"] = acc2
                if on("dv"):
                    W("ac", acc2)
                    eng.tensor_scalar(stats[:, 24:25], sums[:, 2, 0:1],
                                      -1.0 / 1024, None, op0=ALU.mult)
                    eng.tensor_scalar(stats[:, 25:26], sums[:, 2, 1:2],
                                      1.0 / 1024, None, op0=ALU.mult).then_inc(S["dv"], 1)
                e1 = inc("dv")
                if on("dv"):
                    W("dv", e1)
                    eng.tensor_tensor(stats[:, 26:27], stats[:, 24:25],
                                      stats[:, 24:25], op=ALU.mult).then_inc(S["dv"], 1)
                e2 = inc("dv")
                if on("dv"):
                    W("dv", e2)
                    eng.scalar_tensor_tensor(stats[:, 25:26], stats[:, 25:26],
                                             1e-5, stats[:, 26:27], op0=ALU.add,
                                             op1=ALU.subtract).then_inc(S["dv"], 1)
                e3 = inc("dv")
                if on("ac"):
                    W("dv", e3)
                    eng.activation(stats[:, 27:28], stats[:, 25:26],
                                   AF.Ln).then_inc(S["ac"], 1)
                e4 = inc("ac")
                if on("ac"):
                    W("ac", e4)
                    eng.activation(stats[:, 28:29], stats[:, 27:28], AF.Exp,
                                   scale=-0.5).then_inc(S["ac"], 1)
                e5 = inc("ac")
                if on("dv"):
                    W("ac", e5)
                    W("mm", p3.get("bt_mm", 0))
                    eng.tensor_scalar(ptbF[:, 0:1024], tm4[:, s3, :],
                                      stats[:, 24:25], stats[:, 28:29],
                                      op0=ALU.add, op1=ALU.mult).then_inc(S["dv"], 1)
                ln2 = inc("dv")
                p3["ln2_last"] = ln2
                p3["acc2_last"] = acc2
                p3["xfree"][s3] = (acc2, ln2, wx2)
                if on("mm"):
                    W("dv", ln2)
                    W("ac", p3["btev"])
                    ins = None
                    for dd in range(8):
                        bank = s_a[:, 0, :] if dd < 4 else s_a[:, 1, :]
                        col = 128 * (dd % 4)
                        ins = eng.matmul(bank[:, col:col + 128],
                                         ptbF[:, 128 * dd:128 * dd + 128],
                                         identb[:], start=True, stop=True)
                    ins.then_inc(S["mm"], 1)
                bt = inc("mm")
                p3["bt_mm"] = bt
                if on("ac"):
                    W("mm", bt)
                    eng.activation(
                        qts[:, 0:4, C0 + 128 * s3:C0 + 128 * s3 + 128],
                        s_a[:, 0, :].rearrange("p (cc c) -> p cc c", c=128),
                        AF.Identity)
                    eng.activation(
                        qts[:, 4:8, C0 + 128 * s3:C0 + 128 * s3 + 128],
                        s_a[:, 1, :].rearrange("p (cc c) -> p cc c", c=128),
                        AF.Identity).then_inc(S["ac"], 1)
                p3["btev"] = inc("ac")

        def ln_col8(src8, wait_key, wait_val, dbg_tap=False):
            if on("dv"):
                W(wait_key, wait_val)
                eng.tensor_copy(ptbF[:, 1024:1032], src8).then_inc(S["dv"], 1)
            h1a = inc("dv")
            if on("dv"):
                W("dv", h1a)
                eng.scalar_tensor_tensor(ptbF[:, 1032:1040], ptbF[:, 1024:1032],
                                         -1.0, src8, op0=ALU.mult,
                                         op1=ALU.add).then_inc(S["dv"], 1)
            h1 = inc("dv")
            if on("ac"):
                W("dv", h1)
                eng.activation(stats[:, 40:48], src8, AF.Square).then_inc(S["ac"], 1)
            sq = inc("ac")
            if on("dv"):
                W("ac", sq)
                eng.tensor_copy(ptbF[:, 1040:1048],
                                stats[:, 40:48]).then_inc(S["dv"], 1)
            h2a = inc("dv")
            if on("dv"):
                W("dv", h2a)
                eng.scalar_tensor_tensor(ptbF[:, 1048:1056], ptbF[:, 1040:1048],
                                         -1.0, stats[:, 40:48], op0=ALU.mult,
                                         op1=ALU.add).then_inc(S["dv"], 1)
            h2 = inc("dv")
            if on("mm"):
                W("dv", h2)
                eng.matmul(spare[0:1, 0:8], onescol[:], ptbF[:, 1024:1032],
                           start=True, stop=False)
                eng.matmul(spare[0:1, 0:8], onescol[:], ptbF[:, 1032:1040],
                           start=False, stop=True)
                eng.matmul(spare[0:1, 8:16], onescol[:], ptbF[:, 1040:1048],
                           start=True, stop=False)
                eng.matmul(spare[0:1, 8:16], onescol[:], ptbF[:, 1048:1056],
                           start=False, stop=True).then_inc(S["mm"], 1)
            sm = inc("mm")
            if dbg_tap:
                if on("dv"):
                    W("dv", h2)
                    eng.tensor_copy(c8dbg[:, 32:40], ptbF[:, 1024:1032])
                    eng.tensor_copy(c8dbg[:, 40:48], ptbF[:, 1032:1040])
                    eng.tensor_copy(c8dbg[:, 48:56], ptbF[:, 1040:1048])
                    eng.tensor_copy(c8dbg[:, 56:64],
                                    ptbF[:, 1048:1056]).then_inc(S["dv"], 1)
                inc("dv")
                tp_ev = act_copy(c8dbg[0:1, 64:80], spare[0:1, 0:16],
                                 wait=[("mm", sm)])
            ev = act_copy(srow2[0:1, 0:16], spare[0:1, 0:16], wait=[("mm", sm)])
            if dbg_tap:
                if on("dv"):
                    W("ac", ev)
                    eng.tensor_copy(c8dbg[0:1, 80:96],
                                    srow2[0:1, 0:16]).then_inc(S["dv"], 1)
                inc("dv")
            if on("dv"):
                W("ac", ev)
                eng.tensor_reduce(srow2[0:1, 16:17], srow2[0:1, 0:8],
                                  AX.X, ALU.add)
                eng.tensor_reduce(srow2[0:1, 17:18], srow2[0:1, 8:16],
                                  AX.X, ALU.add).then_inc(S["dv"], 1)
            g1 = inc("dv")
            if on("dv"):
                W("dv", g1)
                eng.tensor_scalar(srow2[0:1, 18:19], srow2[0:1, 16:17],
                                  -1.0 / 1024, None, op0=ALU.mult)
                eng.tensor_scalar(srow2[0:1, 19:20], srow2[0:1, 17:18],
                                  1.0 / 1024, None, op0=ALU.mult).then_inc(S["dv"], 1)
            g2 = inc("dv")
            if dbg_tap:
                if on("dv"):
                    W("dv", g2)
                    eng.tensor_copy(c8dbg[0:1, 96:104],
                                    srow2[0:1, 16:24]).then_inc(S["dv"], 1)
                g2 = inc("dv")
            if on("dv"):
                W("dv", g2)
                eng.tensor_tensor(srow2[0:1, 20:21], srow2[0:1, 18:19],
                                  srow2[0:1, 18:19], op=ALU.mult).then_inc(S["dv"], 1)
            g3 = inc("dv")
            if on("dv"):
                W("dv", g3)
                eng.scalar_tensor_tensor(srow2[0:1, 19:20], srow2[0:1, 19:20],
                                         1e-5, srow2[0:1, 20:21], op0=ALU.add,
                                         op1=ALU.subtract).then_inc(S["dv"], 1)
            g4 = inc("dv")
            if dbg_tap:
                if on("dv"):
                    W("dv", g4)
                    eng.tensor_copy(c8dbg[0:1, 104:112],
                                    srow2[0:1, 16:24]).then_inc(S["dv"], 1)
                g4 = inc("dv")
            if on("ac"):
                W("dv", g4)
                eng.activation(srow2[0:1, 21:22], srow2[0:1, 19:20],
                               AF.Ln).then_inc(S["ac"], 1)
            g5a = inc("ac")
            if on("ac"):
                W("ac", g5a)
                eng.activation(srow2[0:1, 22:23], srow2[0:1, 21:22], AF.Exp,
                               scale=-0.5).then_inc(S["ac"], 1)
            g5 = inc("ac")
            if on("dv"):
                W("ac", g5)
                eng.tensor_tensor(srow2[0:1, 23:24], srow2[0:1, 18:19],
                                  srow2[0:1, 22:23], op=ALU.mult).then_inc(S["dv"], 1)
            g6 = inc("dv")
            if dbg_tap:
                if on("dv"):
                    W("dv", g6)
                    eng.tensor_copy(c8dbg[0:1, 112:120],
                                    srow2[0:1, 16:24]).then_inc(S["dv"], 1)
                g6 = inc("dv")
            if on("dv"):
                W("dv", g6)
                eng.tensor_copy(colvb[0:1, 0:2],
                                srow2[0:1, 22:24]).then_inc(S["dv"], 1)
            g7a = inc("dv")
            if dbg_tap:
                if on("dv"):
                    W("dv", g7a)
                    eng.tensor_copy(c8dbg[0:1, 26:28],
                                    colvb[0:1, 0:2]).then_inc(S["dv"], 1)
                g7a = inc("dv")
            if on("dv"):
                W("dv", g7a)
                eng.tensor_copy(srow[0:1, 64:66],
                                colvb[0:1, 0:2]).then_inc(S["dv"], 1)
            g7 = inc("dv")
            if on("dv"):
                W("dv", g7)
                eng.tensor_tensor(srow[0:1, 66:68], srow2[0:1, 22:24],
                                  srow[0:1, 64:66], op=ALU.subtract).then_inc(S["dv"], 1)
            g8 = inc("dv")
            if on("dv"):
                W("dv", g8)
                eng.tensor_copy(colvb[0:1, 2:4],
                                srow[0:1, 66:68]).then_inc(S["dv"], 1)
            g9 = inc("dv")
            if dbg_tap:
                if on("dv"):
                    W("dv", g9)
                    eng.tensor_copy(c8dbg[0:1, 120:128],
                                    srow2[0:1, 16:24])
                    eng.tensor_copy(c8dbg[0:1, 28:32],
                                    colvb[0:1, 0:4]).then_inc(S["dv"], 1)
                g9 = inc("dv")
            if on("mm"):
                W("dv", g9)
                eng.matmul(spare[:, 16:18], onesb[:], colvb[0:1, 0:2],
                           start=True, stop=False)
                eng.matmul(spare[:, 16:18], onesb[:], colvb[0:1, 2:4],
                           start=False, stop=True).then_inc(S["mm"], 1)
            bcm = inc("mm")
            bce = act_copy(stats[:, 30:32], spare[:, 16:18], wait=[("mm", bcm)])
            if dbg_tap and on("dv"):
                W("ac", bce)
                eng.tensor_copy(c8dbg[:, 16:18], stats[:, 30:32])
                eng.tensor_copy(c8dbg[0:1, 18:26],
                                srow2[0:1, 16:24]).then_inc(S["dv"], 1)
            if dbg_tap:
                bce2 = inc("dv")
            else:
                bce2 = None
            if on("dv"):
                if bce2 is not None:
                    W("dv", bce2)
                else:
                    W("ac", bce)
                eng.tensor_scalar(src8, src8, stats[:, 30:31], stats[:, 31:32],
                                  op0=ALU.mult, op1=ALU.add).then_inc(S["dv"], 1)
            return inc("dv")

        if on("mm"):
            W("dv", p3["bankrd"])
            ins = None
            for dd in range(8):
                for cc in range(8):
                    ins = eng.matmul(bndp[:, dd:dd + 1],
                                     w3v[:, 0, cc, 128 * dd:128 * dd + 128],
                                     atn[:, cc, 2048:2049],
                                     start=(cc == 0), stop=(cc == 7))
            ins.then_inc(S["mm"], 1)
        c8mm = inc("mm")
        c8ev = act_copy(stats[:, 32:40], bndp[:, 0:8], wait=[("mm", c8mm)])
        if on("dv"):
            W("ac", c8ev)
            eng.tensor_tensor(stats[:, 32:40], stats[:, 32:40], vecs[:, 24:32],
                              op=ALU.add).then_inc(S["dv"], 1)
        c8b = inc("dv")
        c8ln = ln_col8(stats[:, 32:40], "dv", c8b)
        def W_gp_all():
            for i_ in range(NSLOT):
                nm = f"gp{i_}"
                if slot_uses[nm] > 0:
                    W(nm, 16 * slot_uses[nm])

        if on("gp"):
            W_gp_all()
            W("dv", p3["ln2_last"])
            W("ac", p3["acc2_last"])
        wxe = dma("gp", tm4[0:8, 1, 0:128],
                  dataclasses.replace(xsd[0:1, 0:1024],
                                      offset=xsd.offset + 2048 * 1024,
                                      ap=[[128, 8], [1, 128]]))
        if on("dv"):
            W(wxe)
            W("dv", c8ln)
            eng.tensor_copy(ptbF[0:8, 1024:1152],
                            tm4[0:8, 1, 0:128]).then_inc(S["dv"], 1)
        xh1a = inc("dv")
        if on("dv"):
            W("dv", xh1a)
            eng.scalar_tensor_tensor(ptbF[0:8, 1152:1280], ptbF[0:8, 1024:1152],
                                     -1.0, tm4[0:8, 1, 0:128], op0=ALU.mult,
                                     op1=ALU.add).then_inc(S["dv"], 1)
        xh1 = inc("dv")
        if on("mm"):
            W("dv", xh1)
            eng.matmul(spare[:, 24:32], ptbF[0:8, 1024:1152],
                       identb[0:8, 0:8], start=True, stop=False)
            eng.matmul(spare[:, 24:32], ptbF[0:8, 1152:1280],
                       identb[0:8, 0:8], start=False, stop=True).then_inc(S["mm"], 1)
        xtm = inc("mm")
        xte = act_copy(tm4[:, 0, 0:8], spare[:, 24:32], wait=[("mm", xtm)])
        if on("dv"):
            W("ac", xte)
            eng.tensor_tensor(tm4[:, 0, 0:8], stats[:, 32:40], tm4[:, 0, 0:8],
                              op=ALU.add).then_inc(S["dv"], 1)
        c8x2 = inc("dv")
        if on("dv"):
            W("dv", c8x2)
            eng.tensor_copy(ptbF[:, 1024:1032],
                            tm4[:, 0, 0:8]).then_inc(S["dv"], 1)
        xh2a = inc("dv")
        if on("dv"):
            W("dv", xh2a)
            eng.scalar_tensor_tensor(ptbF[:, 1032:1040], ptbF[:, 1024:1032],
                                     -1.0, tm4[:, 0, 0:8], op0=ALU.mult,
                                     op1=ALU.add).then_inc(S["dv"], 1)
        xh2 = inc("dv")
        if on("mm"):
            W("dv", xh2)
            eng.matmul(spare[0:8, 32:160], ptbF[:, 1024:1032],
                       identb[:], start=True, stop=False)
            eng.matmul(spare[0:8, 32:160], ptbF[:, 1032:1040],
                       identb[:], start=False, stop=True).then_inc(S["mm"], 1)
        xtm2 = inc("mm")
        xte2 = act_copy(tm4[0:8, 1, 0:128], spare[0:8, 32:160],
                        wait=[("mm", xtm2)])
        if on("gp"):
            W("ac", xte2)
        wx2e = dma("gp",
                   dataclasses.replace(x2d[0:1, 0:1024],
                                       offset=x2d.offset + 2048 * 1024,
                                       ap=[[128, 8], [1, 128]]),
                   tm4[0:8, 1, 0:128])
        if on("dv"):
            W("dv", c8x2)
            W("mm", xtm2)
            eng.tensor_copy(stats[:, 32:40], tm4[:, 0, 0:8]).then_inc(S["dv"], 1)
        c8cp = inc("dv")
        c8ln2 = ln_col8(stats[:, 32:40], "dv", c8cp)
        qcol_out = dataclasses.replace(qts[:, :, 2048:2049],
                                       ap=[[8 * 2052, 128], [2052, 8]])
        if on("dv"):
            W("dv", c8ln2)
            eng.tensor_copy(qcol_out, stats[:, 32:40]).then_inc(S["dv"], 1)
        c8st = inc("dv")
        mcol = dataclasses.replace(qts[:, :, 0:1], ap=[[8 * 2052, 128], [2052, 8]])
        if on("dv"):
            W("dv", c8st)
            W("ac", p3["btev"])
            eng.tensor_scalar(mcol, mcol, vecs[:, 48:49], None,
                              op0=ALU.mult).then_inc(S["dv"], 1)
        maskd = inc("dv")

        if on("dv"):
            W("dv", maskd)
            eng.tensor_scalar(stats[:, 0:8], vecs[:, 32:40], -1.0, 1.0,
                              op0=ALU.mult, op1=ALU.add)
            eng.tensor_scalar(stats[:, 8:16], vecs[:, 40:48], -1.0, 1.0,
                              op0=ALU.mult, op1=ALU.add).then_inc(S["dv"], 1)
        onem_d = inc("dv")

        f3 = {"xr": 0, "kact": 0, "hilo": 0, "ytt": 0, "ydma": [],
              "x2rdw": None, "tp_last": 0, "bankrd": 0,
              "opev": {0: 0, 1: 0, 2: 0, 3: 0}, "rfree": 0}

        for t3 in range(4):
            c1 = 1 + 512 * t3
            if on("gp"):
                W("mm", c8mm)
                W("dv", f3["ytt"])
                W_gp_all()
            wx2r = dma("gp", x2rd.rearrange("p (a c) -> p a c", c=1024),
                       x2d[c1:c1 + 512, :].rearrange("(a p) c -> p a c", p=128))
            f3["x2rdw"] = wx2r
            if on("dv"):
                W("dv", onem_d)
                if t3 == 0:
                    W("dv", p3["hilo"])
                    W("mm", p3["tp_last"])
                else:
                    W("mm", f3["tp_last"])
                ins = None
                for cc in range(8):
                    eng.tensor_scalar(xrv[:, cc, :], qts[:, cc, c1:c1 + 512],
                                      vecs[:, 32 + cc:33 + cc], None,
                                      op0=ALU.mult)
                    ins = eng.tensor_scalar(xkv[:, cc, :],
                                            qts[:, cc, c1:c1 + 512],
                                            vecs[:, 40 + cc:41 + cc], None,
                                            op0=ALU.mult)
                ins.then_inc(S["dv"], 1)
            mix_a = inc("dv")
            if on("dv"):
                W("dv", mix_a)
                ins = None
                for cc in range(8):
                    eng.scalar_tensor_tensor(xrv[:, cc, :],
                                             qts[:, cc, c1 - 1:c1 + 511],
                                             stats[:, cc:cc + 1],
                                             xrv[:, cc, :], op0=ALU.mult,
                                             op1=ALU.add)
                    ins = eng.scalar_tensor_tensor(xkv[:, cc, :],
                                                   qts[:, cc, c1 - 1:c1 + 511],
                                                   stats[:, 8 + cc:9 + cc],
                                                   xkv[:, cc, :], op0=ALU.mult,
                                                   op1=ALU.add)
                ins.then_inc(S["dv"], 1)
            f3["xr"] = inc("dv")
            for dd in range(8):
                bank = banksl[dd % 4]
                if on("mm"):
                    if dd == 0:
                        W("dv", f3["xr"])
                        if t3 == 0:
                            W("ac", p3["btev"])
                            for t_ in w3l[1:2]:
                                W(t_)
                        else:
                            W("dv", f3["vev7"])
                    W("ac", f3["opev"][dd % 4])
                    ins = None
                    for cc in range(8):
                        ins = eng.matmul(bank,
                                         w3v[:, 1, cc, 128 * dd:128 * dd + 128],
                                         xrv[:, cc, :], start=(cc == 0),
                                         stop=(cc == 7))
                    ins.then_inc(S["mm"], 1)
                mmid = inc("mm")
                if on("ac"):
                    W("mm", mmid)
                    if dd == 0:
                        W("dv", f3["rfree"])
                        for t_ in f3["ydma"][-4:]:
                            W(t_)
                    eng.activation(rfv[:, dd, :], bank,
                                   AF.Sigmoid).then_inc(S["ac"], 1)
                f3["opev"][dd % 4] = inc("ac")
            kacts = {}
            for dd in range(8):
                bank = banksl[dd % 4]
                if on("mm"):
                    if dd == 0:
                        for t_ in w3l[2:3]:
                            W(t_)
                    W("ac", f3["opev"][dd % 4])
                    if dd >= 4:
                        W("dv", kacts[dd - 4])
                    ins = None
                    for cc in range(8):
                        ins = eng.matmul(bank,
                                         w3v[:, 2, cc, 128 * dd:128 * dd + 128],
                                         xkv[:, cc, :], start=(cc == 0),
                                         stop=(cc == 7))
                    ins.then_inc(S["mm"], 1)
                mmid = inc("mm")
                if on("dv"):
                    W("mm", mmid)
                    eng.tensor_scalar(xrv[:, dd, :], bank, 0.0, None,
                                      op0=ALU.max).then_inc(S["dv"], 1)
                krl = inc("dv")
                if on("dv"):
                    W("dv", krl)
                    eng.tensor_tensor(xrv[:, dd, :], xrv[:, dd, :],
                                      xrv[:, dd, :],
                                      op=ALU.mult).then_inc(S["dv"], 1)
                kacts[dd] = inc("dv")
                f3["opev"][dd % 4] = 0
            f3["kact"] = kacts[7]
            vevs = {}
            for dd in range(8):
                bank = banksl[dd % 4]
                if on("mm"):
                    if dd == 0:
                        W("dv", f3["kact"])
                        for t_ in w3l[3:4]:
                            W(t_)
                    if dd >= 4:
                        W("dv", vevs[dd - 4])
                    ins = None
                    for cc in range(8):
                        ins = eng.matmul(bank,
                                         w3v[:, 3, cc, 128 * dd:128 * dd + 128],
                                         xrv[:, cc, :], start=(cc == 0),
                                         stop=(cc == 7))
                    ins.then_inc(S["mm"], 1)
                mmid = inc("mm")
                if on("dv"):
                    W("mm", mmid)
                    if dd == 0 and t3 == 0:
                        W("dv", p3["ln2_last"])
                        W("ac", p3["acc2_last"])
                        W(wx2e)
                        for t_ in [p3["x2slot"]]:
                            W(t_)
                    eng.tensor_tensor(tm8[:, dd, :], bank, rfv[:, dd, :],
                                      op=ALU.mult).then_inc(S["dv"], 1)
                vevs[dd] = inc("dv")
            f3["vev7"] = vevs[7]
            vmm_last = cnt["mm"]
            if on("dv"):
                W("dv", vevs[7])
                W("mm", vmm_last)
                eng.tensor_copy(xrv[:, :, :], tm8[:, :, :]).then_inc(S["dv"], 1)
            fhi_a = inc("dv")
            if on("dv"):
                W("dv", fhi_a)
                eng.scalar_tensor_tensor(xkv[:, :, :], xrv[:, :, :], -1.0,
                                         tm8[:, :, :], op0=ALU.mult,
                                         op1=ALU.add).then_inc(S["dv"], 1)
            f3["hilo"] = inc("dv")
            for s3 in range(4):
                if on("mm"):
                    W("dv", f3["hilo"])
                    W("dv", f3["bankrd"])
                    ins = None
                    for dd in range(8):
                        bank = bndp if dd < 4 else spare
                        col = 128 * (dd % 4)
                        eng.matmul(bank[:, col:col + 128],
                                   xrv[:, dd, 128 * s3:128 * s3 + 128],
                                   identb[:], start=True, stop=False)
                        ins = eng.matmul(bank[:, col:col + 128],
                                         xkv[:, dd, 128 * s3:128 * s3 + 128],
                                         identb[:], start=False, stop=True)
                    ins.then_inc(S["mm"], 1)
                tp = inc("mm")
                f3["tp_last"] = tp
                lnp = ln_psum(16, tp)
                if on("dv"):
                    W("dv", lnp)
                    W(wx2r)
                    eng.tensor_tensor(rff[:, 1024 * s3:1024 * s3 + 512],
                                      bndp[:, 0:512],
                                      x2rd[:, 1024 * s3:1024 * s3 + 512],
                                      op=ALU.add)
                    eng.tensor_tensor(rff[:, 1024 * s3 + 512:1024 * s3 + 1024],
                                      spare[:, 0:512],
                                      x2rd[:, 1024 * s3 + 512:1024 * s3 + 1024],
                                      op=ALU.add).then_inc(S["dv"], 1)
                ytt = inc("dv")
                f3["ytt"] = ytt
                f3["bankrd"] = ytt
                if on("gp"):
                    W("dv", ytt)
                wy = dma("gp",
                         y_out[512 * t3 + 128 * s3:512 * t3 + 128 * s3 + 128, :],
                         rff[:, 1024 * s3:1024 * s3 + 1024])
                f3["ydma"].append(wy)
            f3["rfree"] = f3["ytt"]

        keys = (["x2", "qt"] if debug else [])
        return finish(me, eng, cnt, S, keys)

    PH_LT3 = [phases < 4]

    def finish(me, eng, cnt, S, debug_keys):
        last_st = 0
        srcs = {
            "xlnt": xlntd, "kt": ktd, "vsw": vsw, "p8f": p8f, "p8r": p8r,
            "x2": x2d[0:2049, :],
            "qt": qts[:], "sc": srow[:], "vtab": vtab[:], "vtab1": vtab1[:],
            "c8dbg": c8dbg[:],
            "atn": atn[:],
        }
        if me == "sy":
            for k in sem_names:
                if k == "st":
                    continue
                if cnt.get(k, 0) > 0:
                    eng.wait_ge(S[k], cnt[k])
        for key in debug_keys:
            if me == "sy":
                if last_st:
                    eng.wait_ge(S["st"], last_st)
                eng.dma_start(out=dbg[key], in_=srcs[key]).then_inc(S["st"], 16)
            cnt["st"] += 16
            last_st = cnt["st"]
        if PH_LT3[0]:
            if me == "sy":
                if last_st:
                    eng.wait_ge(S["st"], last_st)
                eng.dma_start(out=y_out[0:128, :],
                              in_=pf32[:, 0:1024]).then_inc(S["st"], 16)
            cnt["st"] += 16
            last_st = cnt["st"]
        if me == "sy" and last_st:
            eng.wait_ge(S["st"], last_st)
        return cnt

    @block.sync
    def _(sync):
        walk("sy", sync)

    @block.gpsimd
    def _(gpsimd):
        walk("gp", gpsimd)

    @block.tensor
    def _(tensor):
        walk("mm", tensor)

    @block.scalar
    def _(scalar):
        walk("ac", scalar)

    @block.vector
    def _(vector):
        walk("dv", vector)

    ctx.close()
    return nc



def _prep_inputs(inputs):
    x = np.ascontiguousarray(inputs["L0_attn_x"])
    tw = lambda w: np.ascontiguousarray(np.asarray(w).T).astype(np.float32, copy=False)
    wq, wk, wv, wo = (tw(inputs["L0_attn_wq"]), tw(inputs["L0_attn_wk"]),
                      tw(inputs["L0_attn_wv"]), tw(inputs["L0_attn_wo"]))
    wr, wkf, wvf = (tw(inputs["L0_ffn_W_r"]), tw(inputs["L0_ffn_W_k"]),
                    tw(inputs["L0_ffn_W_v"]))
    relb = np.ascontiguousarray(np.asarray(inputs["L0_attn_rel_bias"],
                                           np.float32))
    mf = np.ascontiguousarray(_m_fwd()[:, ::-1])
    ident = np.eye(128).astype(ml_dtypes.bfloat16)

    def chunkvec(v):
        return np.ascontiguousarray(np.asarray(v, np.float32).reshape(8, 128).T)

    vecs = np.zeros((128, 56), np.float32)
    vecs[:, 0:8] = chunkvec(inputs["L0_attn_bq"])
    vecs[:, 8:16] = chunkvec(inputs["L0_attn_bk"])
    vecs[:, 16:24] = chunkvec(inputs["L0_attn_bv"])
    vecs[:, 24:32] = chunkvec(inputs["L0_attn_bo"])
    vecs[:, 32:40] = chunkvec(inputs["L0_ffn_mu_r"])
    vecs[:, 40:48] = chunkvec(inputs["L0_ffn_mu_k"])
    bvrow = np.ascontiguousarray(
        np.asarray(inputs["L0_attn_bv"], np.float32).reshape(1, D))

    in_maps = []
    for c in range(NCORES):
        b, j = divmod(c, 4)
        im = {
            "xb": np.ascontiguousarray(x[b]).astype(np.float32, copy=False),
            "wqT": wq, "wkT": wk, "wvT": wv, "woT": wo,
            "wrT": wr, "wkfT": wkf, "wvfT": wvf,
            "vecs": np.ascontiguousarray(vecs.copy()),
            "bvrow": bvrow,
            "relb": relb,
            "mf": mf,
            "mr": _m_rev(2048 * j),
            "clsm": _clsmask(j),
            "identb": ident,
        }
        m = im["vecs"]
        m[:, 48] = 0.0 if j == 0 else 1.0
        in_maps.append(im)
    return in_maps


def kernel(**inputs):
    in_maps = _prep_inputs(inputs)
    nc = build_program(debug=DEBUG, phases=PHASES)
    res = run_bass_kernel_spmd(nc, in_maps, core_ids=list(range(NCORES)))
    out = np.zeros((2, L, D), np.float32)
    for c in range(NCORES):
        b, j = divmod(c, 4)
        out[b, 2048 * j:2048 * j + 2048, :] = res.results[c]["y"]
    if DEBUG:
        kernel._last = res
    return out
